# version 10
# speedup vs baseline: 1.3320x; 1.0844x over previous
import numpy as np
import ml_dtypes
import concourse.bass as bass
import concourse.mybir as mybir
from concourse.bass_utils import run_bass_kernel_spmd

F32 = mybir.dt.float32
BF16 = mybir.dt.bfloat16
AF = mybir.ActivationFunctionType
ALU = mybir.AluOpType
AX = mybir.AxisListType

N_CORES = 8
B = 8
S = 4096
SC = S // N_CORES
D = 1024
H0 = 8
DK = 128
FF = 4096
H1 = 16
E1 = 64
EXP = 8
FM = 1024
TPC = B * SC
NT = TPC // 128
LN_EPS = 1e-5
RMS_EPS = 1e-8
QSCALE = 1.0 / np.sqrt(128.0)


class Sem:
    def __init__(self, handle):
        self.h = handle
        self.n = 0


class Emit:

    def __init__(self, nc, ctx):
        self.nc = nc
        self.ctx = ctx
        self.sems = {}
        for k in ("v", "s", "t", "g", "cc"):
            self.sem(k)
        self.last_wait = {}

    def sem(self, key):
        if key not in self.sems:
            self.sems[key] = Sem(self.ctx.enter_context(self.nc.semaphore("sem_" + key)))
        return self.sems[key]

    def eng(self, name):
        return {"v": self.nc.vector, "s": self.nc.scalar, "t": self.nc.tensor,
                "g": self.nc.gpsimd, "y": self.nc.sync}[name]

    def fin(self, sem_key, instr, k=1):
        sem = self.sem(sem_key)
        instr.then_inc(sem.h, k)
        sem.n += k
        return sem.n

    def n(self, sem_key):
        return self.sem(sem_key).n

    def wait(self, waiter_key, sem_key, val=None):
        sem = self.sem(sem_key)
        if val is None:
            val = sem.n
        if val <= 0:
            return
        key = (waiter_key, sem_key)
        if self.last_wait.get(key, -1) >= val:
            return
        self.eng(waiter_key).wait_ge(sem.h, val)
        self.last_wait[key] = val

    def barrier(self, waiters=("v", "s", "t", "g", "y"), producers=None):
        if producers is None:
            producers = list(self.sems.keys())
        for w in waiters:
            for p in producers:
                if p != w and p != "cc":
                    self.wait(w, p)


def rope_tables(positions):
    inv_freq = 1.0 / (10000.0 ** (np.arange(0, DK, 2, dtype=np.float64) / DK))
    freqs = positions[:, None].astype(np.float64) * inv_freq[None, :]
    emb = np.concatenate([freqs, freqs], axis=-1)
    cos = np.cos(emb).astype(np.float32)
    sin = np.sin(emb).astype(np.float32)
    sgn = sin.copy()
    sgn[:, :DK // 2] = -sin[:, :DK // 2]
    return cos, sgn


class L0:

    def __init__(self, nc, em, ctx, n_cores=N_CORES, NB=B, stop_after=None,
                 dbg=(), x1_internal=False):
        self.nc, self.em, self.ctx = nc, em, ctx
        self.x1_internal = x1_internal
        self.n_cores, self.NB = n_cores, NB
        self.stop_after = stop_after
        self.dbg = set(dbg)
        self.NTB = SC // 128
        self.NTT = NB * self.NTB
        self.KVW = NB * H0 * 129
        self.lnp = 0
        self.ln_war = {}

    def declare_io(self):
        nc, NB = self.nc, self.NB
        g = lambda n, sh, dt=F32, kind="ExternalInput": nc.dram_tensor(n, sh, dt, kind=kind).ap()
        self.x0 = g("x0", [NB * SC, D])
        self.xh = g("xh", [32, D])
        self.wqt = g("wqt", [D, D]); self.wkt = g("wkt", [D, D])
        self.wvt = g("wvt", [D, D]); self.wgt = g("wgt", [D, D])
        self.wot = g("wot", [D, D])
        self.wupT = g("wupT", [D, FF]); self.wdnT = g("wdnT", [FF, D])
        self.lepw = g("lepw", [128, 40]); self.lepb = g("lepb", [128, 8])
        self.costab = g("costab", [SC, DK]); self.sgntab = g("sgntab", [SC, DK])
        self.idb = g("idb", [128, 128], BF16)
        self.onesrow = g("onesrow", [1, 128], BF16)
        self.onescol = g("onescol", [128, 1], BF16)
        self.onesrow_f = g("onesrow_f", [1, 128], F32)
        self.idf = g("idf", [128, 128], F32)
        self.cst = g("cst", [128, 4], F32)
        if self.x1_internal:
            self.x1o = nc.dram_tensor("x1o", [NB * SC, D], F32).ap()
        else:
            self.x1o = g("x1o", [NB * SC, D], F32, kind="ExternalOutput")
        self.wup_tern = nc.dram_tensor("wup_tern", [FF // 128, D, 128], BF16).ap()
        self.wdn_tern = nc.dram_tensor("wdn_tern", [D // 128, FF, 128], BF16).ap()
        self.kv_ccin = nc.dram_tensor("kv_ccin", [128, self.KVW], F32)
        self.kv_ccout = nc.dram_tensor("kv_ccout", [128, self.KVW], F32)
        self.vs_ccin = nc.dram_tensor("vs_ccin", [1, self.KVW], F32)
        self.vs_ccout = nc.dram_tensor("vs_ccout", [1, self.KVW], F32)
        self.vs_bf_hbm = nc.dram_tensor("vs_bf_hbm", [1, self.KVW], BF16).ap()
        self.ir0bf_hbm = nc.dram_tensor("ir0bf_hbm", [1, TPC + 32], BF16).ap()
        self.r0bf_hbm = nc.dram_tensor("r0bf_hbm", [1, TPC], BF16).ap()
        for name, sh in (("kv", [128, self.KVW]), ("vs", [1, self.KVW]),
                         ("s0", [NB * SC, D]), ("g0", [NB * SC, D])):
            if name in self.dbg:
                setattr(self, name + "_dbg", g(name + "_dbg", sh, F32, kind="ExternalOutput"))

    def alloc_static(self):
        nc, ctx = self.nc, self.ctx
        NB = self.NB
        sb = lambda name, shape, dt: ctx.enter_context(nc.sbuf_tensor(name, shape, dt))
        self.sbf = sb
        self.xn_d = [sb(f"xn_d{cc}", [128, NB * SC], BF16) for cc in range(8)]
        self.wa = sb("wa", [128, 8192], BF16)
        self.xld = [sb(f"xld{i}", [128, D], F32) for i in range(2)]
        self.stats = sb("stats", [128, 8], F32)
        self.tiny = sb("tiny", [128, 64], F32)
        self.act_scr = sb("act_scr", [128, D], BF16)
        self.act_scr2 = sb("act_scr2", [128, D], BF16)
        self.xn_stage = [sb(f"xnst{i}", [128, D], BF16) for i in range(2)]
        self.ir0c = sb("ir0c", [128, NT], F32)
        self.ir0sc = sb("ir0sc", [128, NT], F32)
        self.irbf = sb("irbf", [128, 1], BF16)
        self.ir0h = sb("ir0h", [128, 1], F32)
        self.ir0hrow_bf = sb("ir0hrow_bf", [1, 32], BF16)
        self.cst_sb = sb("cst_sb", [128, 4], F32)
        self.costab_sb = sb("costab_sb", [128, self.NTB * DK], F32)
        self.sgntab_sb = sb("sgntab_sb", [128, self.NTB * DK], F32)
        self.idb_sb = sb("idb_sb", [128, 128], BF16)
        self.idf_sb = sb("idf_sb", [128, 128], F32)
        self.onesrow_sb = sb("onesrow_sb", [1, 128], BF16)
        self.onescol_sb = sb("onescol_sb", [128, 1], BF16)
        self.onesrow_f_sb = sb("onesrow_f_sb", [1, 128], F32)
        self.lepw_sb = sb("lepw_sb", [128, 40], F32)
        self.lepb_sb = sb("lepb_sb", [128, 8], F32)
        self.xnh = sb("xnh", [128, 8 * 32], BF16)
        self.kscr = sb("kscr", [128, D], F32)
        self.kscr2 = sb("kscr2", [128, D], F32)
        self.rscr = sb("rscr", [128, D], F32)
        self.rscr2 = sb("rscr2", [128, D], F32)
        self.vsumb = sb("vsumb", [128, H0 * 129], BF16)
        self.kv_bf_b = sb("kv_bf_b", [128, H0 * 129], BF16)
        self.t1scr = sb("t1scr", [128, 132], F32)
        self.rden = sb("rden", [128, 1], F32)
        self.mala_tok = sb("mala_tok", [128, 128], BF16)
        self.asum = sb("asum", [128, 40], F32)
        self.alpha_sb = sb("alpha_sb", [128, 1], F32)
        self.nalpha_sb = sb("nalpha_sb", [128, 1], F32)
        self.ir0x = sb("ir0x", [128, 520], BF16)
        self.r0x = sb("r0x", [128, 512], BF16)
        self.r1x = sb("r1x", [128, 512], BF16)
        self.sqh = sb("sqh", [1, 512], F32)
        self.r1row_bf = sb("r1row_bf", [1, 512], BF16)
        self.leph = sb("leph", [128, 512], F32)
        self.oscr = sb("oscr", [128, 128], F32)
        self.dscr = sb("dscr", [128, 512], F32)
        self.sgt = sb("sgt", [128, D], BF16)
        self.stok = sb("stok", [128, D], BF16)
        self.gtok = sb("gtok", [128, D], BF16)
        self.x1t = sb("x1t", [128, D], F32)
        psm = lambda name, dt: ctx.enter_context(nc.psum_tensor(name, [128, 512], dt))
        self.psA = [psm("psA0", F32), psm("psA1", F32)]
        self.psB = [psm("psB0", BF16), psm("psB1", BF16)]
        self.psC = [psm("psC0", F32), psm("psC1", F32), psm("psC2", F32), psm("psC3", F32)]

    def emit(self):
        from contextlib import ExitStack
        self.declare_io()
        self.alloc_static()
        em = self.em
        self.preload()
        self.ternarize_all()
        with ExitStack() as p1:
            sb = lambda name, shape, dt: p1.enter_context(self.nc.sbuf_tensor(name, shape, dt))
            self.wv_sb = sb("wv_sb", [128, 8192], BF16)
            self.krope_b = [sb(f"krope{i}", [128, D], BF16) for i in range(self.NTB)]
            self.vaug_b = [sb(f"vaug{i}", [128, H0 * 129], BF16) for i in range(self.NTB)]
            self.kv_stage = sb("kv_stage", [128, H0 * 129], F32)
            self.vsum_stage = sb("vsum_stage", [1, H0 * 129], F32)
            self.rowstage = sb("rowstage", [1, 512], F32)
            self.rowstage2 = sb("rowstage2", [1, 512], F32)
            self.rowbf0 = sb("rowbf0", [1, 512], BF16)
            self.rowbf1 = sb("rowbf1", [1, 512], BF16)
            self.pass1()
            em.barrier()
        if self.stop_after == "pass1":
            return
        self.collective()
        if self.stop_after == "cc":
            em.barrier()
            return
        with ExitStack() as p2:
            sb = lambda name, shape, dt: p2.enter_context(self.nc.sbuf_tensor(name, shape, dt))
            self.qT_b = sb("qT_b", [128, H0 * SC], BF16)
            self.vcm_b = sb("vcm_b", [128, H0 * 516], BF16)
            self.g_b = sb("g_b", [128, H0 * SC], BF16)
            self.o_b = sb("o_b", [128, H0 * SC], BF16)
            self.s0_b = self.qT_b
            self.g0_b = self.vcm_b
            hex_ = [sb(f"hsb{fc}", [128, 512], BF16) for fc in range(16)]
            self.h_sb = []
            for fc in range(32):
                if fc < 8:
                    self.h_sb.append(self.g_b[:, 512 * fc:512 * (fc + 1)])
                elif fc < 16:
                    self.h_sb.append(self.o_b[:, 512 * (fc - 8):512 * (fc - 7)])
                else:
                    self.h_sb.append(hex_[fc - 16][:, :])
            self.h2scr = [sb(f"h2scr{i}", [128, 512], BF16) for i in range(2)]
            self.wstream = [sb(f"wstr{i}", [128, 1024], BF16) for i in range(2)]
            self.wds = [sb(f"wds{i}", [128, 512], BF16) for i in range(2)]
            self.rowld_ir = sb("rowld_ir", [1, 516], BF16)
            self.rowld_r0 = sb("rowld_r0", [1, 512], BF16)
            self.rowld_vs = sb("rowld_vs", [1, 1032], BF16)
            self.halo_prep()
            self.rope_rows_finish()
            for b in range(self.NB):
                self.pass2_b(b)
            em.barrier()

    def preload(self):
        em = self.em
        y = self.nc.sync
        g = self.nc.gpsimd
        em.fin("dcst", y.dma_start(self.costab_sb[:].rearrange("p (i k) -> p i k", k=DK),
                                 self.costab.rearrange("(i p) k -> p i k", p=128)), 16)
        em.fin("dcst", y.dma_start(self.sgntab_sb[:].rearrange("p (i k) -> p i k", k=DK),
                                 self.sgntab.rearrange("(i p) k -> p i k", p=128)), 16)
        em.fin("dcst", y.dma_start(self.idb_sb[:], self.idb), 16)
        em.fin("dcst", y.dma_start(self.idf_sb[:], self.idf), 16)
        em.fin("dcst", y.dma_start(self.onesrow_sb[:], self.onesrow), 16)
        em.fin("dcst", y.dma_start(self.onescol_sb[:], self.onescol), 16)
        em.fin("dcst", y.dma_start(self.onesrow_f_sb[:], self.onesrow_f), 16)
        em.fin("dcst", y.dma_start(self.lepw_sb[:], self.lepw), 16)
        em.fin("dcst", y.dma_start(self.cst_sb[:], self.cst), 16)
        em.fin("dcst", y.dma_start(self.lepb_sb[:], self.lepb), 16)
        self.const_dy = em.n("dcst")
        for cc in range(8):
            em.fin("dwa", g.dma_start(self.wa[:, 1024 * cc:1024 * (cc + 1)],
                                      self.wkt[128 * cc:128 * (cc + 1), :]), 16)
        self.wkv_dg = em.n("dwa")

    def tern_matrix(self, w_dram, rows, cols, out_dram, nsub):
        em, nc = self.em, self.nc
        v, s, y, t = nc.vector, nc.scalar, nc.tensor, nc.gpsimd
        v, s, t = nc.vector, nc.scalar, nc.tensor
        y = nc.sync
        nch = rows // 128
        ntot = nch * nsub
        assert ntot <= 40
        subw = cols // nsub
        assert subw <= 1024
        idx = 0
        for ch in range(nch):
            for su in range(nsub):
                em.wait("y", "s", getattr(self, "tg_s", 0))
                em.wait("y", "v", getattr(self, "tg_v", 0))
                nd = em.fin("dtw", y.dma_start(
                    self.kscr[:, 0:subw],
                    w_dram[128 * ch:128 * (ch + 1), subw * su:subw * (su + 1)]), 16)
                em.wait("s", "dtw", nd)
                em.wait("s", "dts", em.n("dts"))
                self.tg_s = em.fin("s", s.activation(
                    self.act_scr[:, 0:subw], self.kscr[:, 0:subw],
                    AF.Abs, accum_out=self.asum[:, idx:idx + 1]))
                idx += 1
        em.wait("v", "s", self.tg_s)
        nv = em.fin("v", v.tensor_reduce(self.tiny[:, 32:33], self.asum[:, 0:ntot],
                                         axis=AX.X, op=ALU.add))
        em.wait("v", "v", nv)
        nv = em.fin("v", v.tensor_scalar(self.tiny[:, 33:34], self.tiny[:, 32:33],
                                         0.5 / (rows * cols), None, ALU.mult))
        em.wait("t", "v", nv)
        em.wait("t", "dcst", self.const_dy)
        nt = em.fin("t", t.transpose(self.psA[0][0:1, 0:128], self.tiny[:, 33:34], self.idf_sb[:]))
        em.wait("v", "t", nt)
        nv = em.fin("v", v.tensor_reduce(self.tiny[0:1, 34:35], self.psA[0][0:1, 0:128],
                                         axis=AX.X, op=ALU.add))
        em.wait("t", "v", nv)
        nt = em.fin("t", t.matmul(self.psA[1][:, 0:1], self.onesrow_f_sb[:],
                                  self.tiny[0:1, 34:35], start=True, stop=True))
        em.wait("v", "t", nt)
        em.fin("v", v.tensor_copy(self.alpha_sb[:], self.psA[1][:, 0:1]))
        em.wait("v", "v")
        alpha_v = em.fin("v", v.tensor_scalar(self.nalpha_sb[:], self.alpha_sb[:], -1.0, None, ALU.mult))
        em.wait("t", "v", alpha_v)
        tern_war_d = 0
        for ch in range(nch):
            for su in range(nsub):
                em.wait("y", "v", getattr(self, "tg_v", 0))
                em.wait("y", "s", getattr(self, "tg_s", 0))
                nd = em.fin("dtw", y.dma_start(
                    self.kscr[:, 0:subw],
                    w_dram[128 * ch:128 * (ch + 1), subw * su:subw * (su + 1)]), 16)
                em.wait("v", "dtw", nd)
                em.wait("v", "v", alpha_v)
                em.wait("v", "dts", tern_war_d)
                em.fin("v", v.tensor_scalar(self.rscr[:, 0:subw], self.kscr[:, 0:subw],
                                            self.alpha_sb[:, 0:1], None, ALU.is_gt))
                em.fin("v", v.tensor_scalar(self.rscr2[:, 0:subw], self.kscr[:, 0:subw],
                                            self.nalpha_sb[:, 0:1], None, ALU.is_lt))
                em.wait("v", "v")
                self.tg_v = em.fin("v", v.tensor_tensor(
                    self.act_scr[:, 0:subw], self.rscr[:, 0:subw],
                    self.rscr2[:, 0:subw], op=ALU.subtract))
                em.wait("y", "v", self.tg_v)
                nblk = subw // 128
                blk0 = su * nblk
                dst = out_dram[blk0:blk0 + nblk, 128 * ch:128 * (ch + 1), :]
                dst = dst.rearrange("blk p f -> p blk f")
                srcv = self.act_scr[:, 0:subw].rearrange("p (blk f) -> p blk f", f=128)
                tern_war_d = em.fin("dts", y.dma_start(dst, srcv), 16)
        self.tern_done_dy = em.n("dts")
        self.tern_done_v = em.n("v")

    def ternarize_all(self):
        self.tern_matrix(self.wupT, D, FF, self.wup_tern, 4)
        self.tern_matrix(self.wdnT, FF, D, self.wdn_tern, 1)

    def ln_tile(self, xsrc, nrows, ti=None):
        em, nc = self.em, self.nc
        v, s, y = nc.vector, nc.scalar, nc.sync
        p = self.lnp
        self.lnp ^= 1
        war = self.ln_war.get(p)
        if war is not None:
            em.wait("y", "s", war["s"])
            em.wait("y", "v", war["v"])
        nd = em.fin(f"dx{p}", y.dma_start(self.xld[p][:nrows, :], xsrc), 16)
        em.wait("s", f"dx{p}", nd)
        st, T = self.stats, self.tiny
        c = 16 * p
        em.fin("s", s.activation(self.act_scr[:nrows, :], self.xld[p][:nrows, :],
                                 AF.Identity, accum_out=st[:nrows, 4 * p:4 * p + 1]))
        ns = em.fin("s", s.activation(self.act_scr2[:nrows, :], self.xld[p][:nrows, :],
                                      AF.Square, accum_out=st[:nrows, 4 * p + 1:4 * p + 2]))
        em.wait("v", "s", ns)
        em.fin("v", v.tensor_scalar(T[:nrows, c:c + 1], st[:nrows, 4 * p:4 * p + 1], 1.0 / D, None, ALU.mult))
        em.fin("v", v.tensor_scalar(T[:nrows, c + 1:c + 2], st[:nrows, 4 * p + 1:4 * p + 2], 1.0 / D, None, ALU.mult))
        em.wait("v", "v")
        em.fin("v", v.tensor_tensor(T[:nrows, c + 2:c + 3], T[:nrows, c:c + 1], T[:nrows, c:c + 1], op=ALU.mult))
        em.wait("v", "v")
        nv = em.fin("v", v.tensor_tensor(T[:nrows, c + 3:c + 4], T[:nrows, c + 1:c + 2], T[:nrows, c + 2:c + 3], op=ALU.subtract))
        em.wait("s", "v", nv)
        ns = em.fin("s", s.activation(T[:nrows, c + 4:c + 5], T[:nrows, c + 3:c + 4], AF.Sqrt, bias=self.cst_sb[:nrows, 0:1]))
        em.wait("v", "s", ns)
        nv = em.fin("v", v.reciprocal(T[:nrows, c + 5:c + 6], T[:nrows, c + 4:c + 5]))
        em.wait("v", "v")
        em.fin("v", v.tensor_tensor(T[:nrows, c + 6:c + 7], T[:nrows, c + 3:c + 4], T[:nrows, c + 5:c + 6], op=ALU.mult))
        em.wait("v", "v")
        nv = em.fin("v", v.tensor_tensor(T[:nrows, c + 7:c + 8], T[:nrows, c + 6:c + 7], T[:nrows, c + 5:c + 6], op=ALU.mult))
        em.wait("s", "v", nv)
        ns = em.fin("s", s.activation(T[:nrows, c + 8:c + 9], T[:nrows, c + 7:c + 8], AF.Sqrt))
        ir0_dst = self.ir0c[:nrows, ti:ti + 1] if ti is not None else self.ir0h[:nrows, 0:1]
        em.wait("s", "s", ns)
        ns = em.fin("s", s.activation(ir0_dst, T[:nrows, c + 8:c + 9], AF.Identity, bias=self.cst_sb[:nrows, 1:2]))
        em.wait("v", "s", ns)
        nv = em.fin("v", v.reciprocal(T[:nrows, c + 9:c + 10], ir0_dst))
        em.wait("v", "v")
        em.fin("v", v.tensor_tensor(T[:nrows, c + 10:c + 11], T[:nrows, c + 5:c + 6], T[:nrows, c + 9:c + 10], op=ALU.mult))
        if ti is not None:
            em.fin("v", v.tensor_scalar(self.ir0sc[:nrows, ti:ti + 1], ir0_dst, QSCALE, None, ALU.mult))
        em.wait("v", "v")
        nv = em.fin("v", v.tensor_scalar(self.xn_stage[p][:nrows, :], self.xld[p][:nrows, :],
                                         T[:nrows, c:c + 1], T[:nrows, c + 10:c + 11], ALU.subtract, ALU.mult))
        self.ln_war[p] = {"s": em.n("s"), "v": nv}
        return p, nv

    def pass1(self):
        em, nc = self.em, self.nc
        v, s, y, t = nc.vector, nc.scalar, nc.sync, nc.tensor
        NTB = self.NTB
        em.wait("v", "v", self.tern_done_v)
        em.wait("s", "dts", em.n("dts"))
        em.wait("v", "dts", em.n("dts"))
        for i in range(NTB):
            va = self.vaug_b[i]
            em.fin("v", v.memset(
                va[:].rearrange("p (h e) -> p h e", e=129)[:, :, 128:129], 1.0))
        self.vaug_ones_v = em.n("v")
        for cc in range(8):
            em.fin("dwv", self.nc.gpsimd.dma_start(self.wv_sb[:, 1024 * cc:1024 * (cc + 1)],
                                                   self.wvt[128 * cc:128 * (cc + 1), :]), 16)
        em.wait("t", "dwv", em.n("dwv"))
        em.wait("t", "dwa", self.wkv_dg)
        em.wait("t", "dcst", self.const_dy)
        for b in range(self.NB):
            for i in range(NTB):
                ti = b * NTB + i
                p, xn_v = self.ln_tile(self.x0[128 * ti:128 * (ti + 1), :], 128, ti=ti)
                em.wait("t", "v", xn_v)
                em.wait("t", "s", getattr(self, "xn_drain_s", 0))
                pb = self.psB[ti % 2]
                for cc in range(8):
                    nt = em.fin("t", t.transpose(pb[:, 128 * (cc % 4):128 * (cc % 4) + 128],
                                                 self.xn_stage[p][:, 128 * cc:128 * (cc + 1)],
                                                 self.idb_sb[:]))
                    if cc % 4 == 3:
                        em.wait("s", "t", nt)
                        for c2 in range(cc - 3, cc + 1):
                            ns = em.fin("s", s.copy(self.xn_d[c2][:, 128 * ti:128 * (ti + 1)],
                                                    pb[:, 128 * (c2 % 4):128 * (c2 % 4) + 128]))
                        if cc == 3:
                            em.wait("t", "s", ns)
                self.xn_drain_s = em.n("s")
                em.wait("v", "s", self.xn_drain_s)
                em.wait("v", "v", getattr(self, "irbf_read_t_guard_v", 0))
                em.wait("v", "t", getattr(self, "irbf_read_t", 0))
                nv = em.fin("v", v.tensor_scalar(self.irbf[:], self.ir0c[:, ti:ti + 1], 1.0, None, ALU.mult))
                em.wait("t", "v", nv)
                nt = em.fin("t", t.matmul(self.psC[3][0:1, 128 * i:128 * (i + 1)],
                                          self.irbf[:], self.idb_sb[:], start=True, stop=True))
                self.irbf_read_t = nt
                em.wait("v", "t", nt)
                em.wait("v", "drow", getattr(self, "rowdma_dy", 0))
                nv = em.fin("v", v.tensor_copy(self.rowstage[0:1, 128 * i:128 * (i + 1)],
                                               self.psC[3][0:1, 128 * i:128 * (i + 1)]))
                em.wait("t", "s", self.xn_drain_s)
                em.wait("t", "s", getattr(self, "kv_drain_s", 0))
                for half in range(2):
                    for cc in range(8):
                        em.fin("t", t.matmul(
                            self.psA[half][:, 0:512],
                            self.xn_d[cc][:, 128 * ti:128 * (ti + 1)],
                            self.wa[:, 1024 * cc + 512 * half:1024 * cc + 512 * (half + 1)],
                            start=(cc == 0), stop=(cc == 7)))
                nt = em.n("t")
                em.wait("s", "t", nt)
                em.wait("s", "v", getattr(self, "rope_read_v", 0))
                for half in range(2):
                    ns = em.fin("s", s.activation(self.kscr[:, 512 * half:512 * (half + 1)],
                                                  self.psA[half][:, 0:512], AF.Identity,
                                                  scale=self.ir0c[:, ti:ti + 1]))
                self.kv_drain_s = ns
                em.wait("v", "s", ns)
                self.rope_read_v = self.emit_rope(self.kscr, i, self.krope_b[i])
                em.wait("t", "s", ns)
                for half in range(2):
                    for cc in range(8):
                        em.fin("t", t.matmul(
                            self.psA[half][:, 0:512],
                            self.xn_d[cc][:, 128 * ti:128 * (ti + 1)],
                            self.wv_sb[:, 1024 * cc + 512 * half:1024 * cc + 512 * (half + 1)],
                            start=(cc == 0), stop=(cc == 7)))
                nt = em.n("t")
                em.wait("s", "t", nt)
                em.wait("s", "v", self.vaug_ones_v)
                dst3 = self.vaug_b[i][:].rearrange("p (h e) -> p h e", e=129)
                for half in range(2):
                    ns = em.fin("s", s.activation(
                        dst3[:, 4 * half:4 * half + 4, 0:128],
                        self.psA[half][:, 0:512].rearrange("p (h e) -> p h e", e=128),
                        AF.Identity, scale=self.ir0c[:, ti:ti + 1]))
                self.kv_drain_s = ns
                em.wait("t", "s", ns)
            em.wait("t", "s", em.n("s"))
            em.wait("t", "v", em.n("v"))
            for hg in range(2):
                for j in range(4):
                    h = 4 * hg + j
                    for tt in range(NTB):
                        nt = em.fin("t", t.matmul(
                            self.psC[j][:, 0:129],
                            self.krope_b[tt][:, 128 * h:128 * (h + 1)],
                            self.vaug_b[tt][:, 129 * h:129 * (h + 1)],
                            start=(tt == 0), stop=(tt == NTB - 1)))
                    em.wait("v", "t", nt)
                    nv = em.fin("v", v.tensor_copy(self.kv_stage[:, 129 * h:129 * (h + 1)],
                                                   self.psC[j][:, 0:129]))
                em.wait("t", "v", nv)
            for j, (o0, sz) in enumerate(((0, 387), (387, 387), (774, 258))):
                for tt in range(NTB):
                    nt = em.fin("t", t.matmul(self.psC[j][0:1, 0:sz], self.onescol_sb[:],
                                              self.vaug_b[tt][:, o0:o0 + sz],
                                              start=(tt == 0), stop=(tt == NTB - 1)))
                em.wait("v", "t", nt)
                nv = em.fin("v", v.tensor_copy(self.vsum_stage[0:1, o0:o0 + sz],
                                               self.psC[j][0:1, 0:sz]))
            em.wait("t", "v", nv)
            em.wait("y", "v", em.n("v"))
            em.fin("dkv", y.dma_start(self.kv_ccin.ap()[:, 1032 * b:1032 * (b + 1)], self.kv_stage[:]), 16)
            em.fin("dkv", y.dma_start(self.vs_ccin.ap()[0:1, 1032 * b:1032 * (b + 1)], self.vsum_stage[0:1, :]), 16)
            em.wait("v", "v")
            em.fin("v", v.reciprocal(self.rowstage2[0:1, :], self.rowstage[0:1, :]))
            em.fin("v", v.tensor_scalar(self.rowbf0[0:1, :], self.rowstage[0:1, :], 1.0, None, ALU.mult))
            em.wait("v", "v")
            nv = em.fin("v", v.tensor_scalar(self.rowbf1[0:1, :], self.rowstage2[0:1, :], 1.0, None, ALU.mult))
            em.wait("y", "v", nv)
            em.fin("drow", y.dma_start(self.ir0bf_hbm[0:1, 512 * b:512 * (b + 1)], self.rowbf0[0:1, :]), 16)
            self.rowdma_dy = em.fin("drow", y.dma_start(self.r0bf_hbm[0:1, 512 * b:512 * (b + 1)], self.rowbf1[0:1, :]), 16)
            em.wait("v", "dkv", em.n("dkv"))
        self.pass1_dy = em.n("dkv")

    def emit_rope(self, src_f32, tile_i, dst_bf):
        em, nc = self.em, self.nc
        v = nc.vector
        cosb = self.costab_sb[:, 128 * tile_i:128 * (tile_i + 1)]
        sgnb = self.sgntab_sb[:, 128 * tile_i:128 * (tile_i + 1)]
        src3 = src_f32[:].rearrange("p (h k) -> p h k", k=128)
        r1 = self.rscr[:].rearrange("p (h k) -> p h k", k=128)
        r2 = self.rscr2[:].rearrange("p (h k) -> p h k", k=128)
        cb = cosb.unsqueeze(1).broadcast_to([128, H0, 128])
        em.wait("v", "v", getattr(self, "rope_scr_guard", 0))
        em.fin("v", v.tensor_tensor(r1, src3, cb, op=ALU.mult))
        slo = sgnb[:, 0:64].unsqueeze(1).broadcast_to([128, H0, 64])
        shi = sgnb[:, 64:128].unsqueeze(1).broadcast_to([128, H0, 64])
        em.fin("v", v.tensor_tensor(r2[:, :, 0:64], src3[:, :, 64:128], slo, op=ALU.mult))
        nv = em.fin("v", v.tensor_tensor(r2[:, :, 64:128], src3[:, :, 0:64], shi, op=ALU.mult))
        em.wait("v", "v", nv)
        nv = em.fin("v", v.tensor_tensor(dst_bf[:], self.rscr[:], self.rscr2[:], op=ALU.add))
        self.rope_scr_guard = nv
        return nv

    def collective(self):
        em, nc = self.em, self.nc
        g = nc.gpsimd
        em.wait("g", "dkv", self.pass1_dy)
        if self.n_cores > 1:
            em.fin("cc", g.collective_compute(
                "AllReduce", ALU.add,
                replica_groups=[list(range(self.n_cores))],
                ins=[self.kv_ccin.ap().opt()],
                outs=[self.kv_ccout.ap().opt()]))
            ncc = em.fin("cc", g.collective_compute(
                "AllReduce", ALU.add,
                replica_groups=[list(range(self.n_cores))],
                ins=[self.vs_ccin.ap().opt()],
                outs=[self.vs_ccout.ap().opt()]))
            em.wait("g", "cc", ncc)
            src_kv, src_vs = self.kv_ccout, self.vs_ccout
        else:
            src_kv, src_vs = self.kv_ccin, self.vs_ccin
        em.fin("dvs", g.dma_start(self.vs_bf_hbm[0:1, :], src_vs.ap()[0:1, :]), 16)
        self.kv_src = src_kv
        self.cc_done = em.n("dvs")
        if "kv" in self.dbg:
            em.fin("ddbg", g.dma_start(self.kv_dbg, src_kv.ap()[:, :]), 16)
        if "vs" in self.dbg:
            em.fin("ddbg", g.dma_start(self.vs_dbg, src_vs.ap()[0:1, :]), 16)

    def halo_prep(self):
        em, nc = self.em, self.nc
        v, s, t = nc.vector, nc.scalar, nc.tensor
        p, xn_v = self.ln_tile(self.xh[0:32, :], 32, ti=None)
        em.wait("t", "v", xn_v)
        pb = self.psB[0]
        for cc in range(8):
            nt = em.fin("t", t.transpose(pb[:, 32 * (cc % 4):32 * (cc % 4) + 32],
                                         self.xn_stage[p][0:32, 128 * cc:128 * (cc + 1)],
                                         self.idb_sb[0:32, 0:32]))
            if cc % 4 == 3:
                em.wait("s", "t", nt)
                for c2 in range(cc - 3, cc + 1):
                    ns = em.fin("s", s.copy(self.xnh[:, 32 * c2:32 * (c2 + 1)],
                                            pb[:, 32 * (c2 % 4):32 * (c2 % 4) + 32]))
                if cc == 3:
                    em.wait("t", "s", ns)
        em.wait("v", "s", em.n("s"))
        em.wait("v", "t", getattr(self, "irbf_read_t", 0))
        nv = em.fin("v", v.tensor_scalar(self.irbf[0:32, :], self.ir0h[0:32, 0:1], 1.0, None, ALU.mult))
        em.wait("t", "v", nv)
        nt = em.fin("t", t.matmul(self.psC[3][0:1, 0:32], self.irbf[0:32, :],
                                  self.idb_sb[0:32, 0:32], start=True, stop=True))
        self.irbf_read_t = nt
        em.wait("v", "t", nt)
        nv = em.fin("v", v.tensor_copy(self.ir0hrow_bf[0:1, :], self.psC[3][0:1, 0:32]))
        self.halo_v = nv
        em.wait("y", "v", nv)
        nd = em.fin("drow", nc.sync.dma_start(self.ir0bf_hbm[0:1, TPC:TPC + 32], self.ir0hrow_bf[0:1, :]), 16)
        self.halo_dy = nd

    def rope_rows_finish(self):
        self.rows_v = 0

    def pass2_b(self, b):
        em, nc = self.em, self.nc
        v, t, g = nc.vector, nc.tensor, nc.gpsimd
        em.wait("g", "dvs", self.cc_done)
        em.wait("g", "dkv", self.pass1_dy)
        if b > 0:
            em.wait("g", "t", getattr(self, "kvbf_last_read_t", 0))
            em.wait("g", "v", getattr(self, "vsumb_last_read_v", 0))
        self.kvbf_dg = em.fin("dkb", g.dma_start(self.kv_bf_b[:], self.kv_src.ap()[:, 1032 * b:1032 * (b + 1)]), 16)
        y = nc.sync
        em.wait("y", "t", getattr(self, "rowld_read_t", 0))
        em.wait("y", "dvs", self.cc_done)
        em.wait("y", "drow", em.n("drow"))
        em.fin("drl", y.dma_start(self.rowld_vs[0:1, :], self.vs_bf_hbm[0:1, 1032 * b:1032 * (b + 1)]), 16)
        em.fin("drl", y.dma_start(self.rowld_ir[0:1, 0:512], self.ir0bf_hbm[0:1, 512 * b:512 * (b + 1)]), 16)
        em.fin("drl", y.dma_start(self.rowld_ir[0:1, 512:516], self.ir0bf_hbm[0:1, TPC + 4 * b:TPC + 4 * (b + 1)]), 16)
        rowd = em.fin("drl", y.dma_start(self.rowld_r0[0:1, :], self.r0bf_hbm[0:1, 512 * b:512 * (b + 1)]), 16)
        em.wait("t", "dvs", self.cc_done)
        em.wait("t", "drl", rowd)
        for w2 in range(2):
            for j in range(4):
                h = 4 * w2 + j
                nt = em.fin("t", t.matmul(self.psC[j][:, 0:129], self.onesrow_sb[:],
                                          self.rowld_vs[0:1, 129 * h:129 * (h + 1)],
                                          start=True, stop=True))
            em.wait("v", "t", nt)
            for j in range(4):
                h = 4 * w2 + j
                nv = em.fin("v", v.tensor_copy(self.vsumb[:, 129 * h:129 * (h + 1)],
                                               self.psC[j][:, 0:129]))
            em.wait("t", "v", nv)
        nt = em.fin("t", t.matmul(self.psA[0][:, 0:512], self.onesrow_sb[:],
                                  self.rowld_r0[0:1, :], start=True, stop=True))
        em.wait("v", "t", nt)
        nv = em.fin("v", v.tensor_copy(self.r0x[:], self.psA[0][:, 0:512]))
        nt = em.fin("t", t.matmul(self.psA[1][:, 0:512], self.onesrow_sb[:],
                                  self.rowld_ir[0:1, 0:512], start=True, stop=True))
        em.wait("t", "v", self.halo_v)
        nt2 = em.fin("t", t.matmul(self.psC[0][:, 0:4], self.onesrow_sb[:],
                                   self.rowld_ir[0:1, 512:516], start=True, stop=True))
        em.wait("v", "t", nt2)
        em.fin("v", v.tensor_copy(self.ir0x[:, 0:512], self.psA[1][:, 0:512]))
        nv = em.fin("v", v.tensor_copy(self.ir0x[:, 512:516], self.psC[0][:, 0:4]))
        em.wait("t", "v", nv)
        self.rowld_read_t = em.n("t")
        self.pass2_attn(b)
        self.pass2_ffn(b)
        self.pass2_x1(b)

    def load_wslot0(self, wdram):
        em, nc = self.em, self.nc
        g = nc.gpsimd
        em.wait("g", "t")
        nd = 0
        for cc in range(8):
            nd = em.fin("dwa", g.dma_start(self.wa[:, 1024 * cc:1024 * (cc + 1)],
                                           wdram[128 * cc:128 * (cc + 1), :]), 16)
        return nd

    def pass2_attn(self, b):
        em, nc = self.em, self.nc
        v, s, y, t = nc.vector, nc.scalar, nc.sync, nc.tensor
        NTB = self.NTB
        nd = self.load_wslot0(self.wqt)
        em.wait("t", "dwa", nd)
        qpair = [(self.psA[0], self.psA[1]), (self.psC[0], self.psC[1])]
        qdrain = {0: 0, 1: 0}
        for i in range(NTB):
            ti = b * NTB + i
            par = i % 2
            pa0, pa1 = qpair[par]
            em.wait("t", "s", qdrain[par])
            for half, pa in ((0, pa0), (1, pa1)):
                for cc in range(8):
                    em.fin("t", t.matmul(
                        pa[:, 0:512],
                        self.xn_d[cc][:, 128 * ti:128 * (ti + 1)],
                        self.wa[:, 1024 * cc + 512 * half:1024 * cc + 512 * (half + 1)],
                        start=(cc == 0), stop=(cc == 7)))
            nt = em.n("t")
            em.wait("s", "t", nt)
            em.wait("s", "v", getattr(self, "rope_read_v", 0))
            kp = self.kscr if par == 0 else self.kscr2
            for half, pa in ((0, pa0), (1, pa1)):
                ns = em.fin("s", s.activation(kp[:, 512 * half:512 * (half + 1)],
                                              pa[:, 0:512], AF.Identity,
                                              scale=self.ir0sc[:, ti:ti + 1]))
            qdrain[par] = ns
            em.wait("v", "s", ns)
            em.wait("v", "s", getattr(self, "qT_drain_s", 0))
            self.rope_read_v = self.emit_rope(kp, i, self.act_scr2)
            em.wait("t", "v", self.rope_read_v)
            pb = self.psB[i % 2]
            for h in range(8):
                nt = em.fin("t", t.transpose(pb[:, 128 * (h % 4):128 * (h % 4) + 128],
                                             self.act_scr2[:, 128 * h:128 * (h + 1)],
                                             self.idb_sb[:]))
                if h % 4 == 3:
                    em.wait("s", "t", nt)
                    for h2 in range(h - 3, h + 1):
                        ns = em.fin("s", s.copy(self.qT_b[:, 512 * h2 + 128 * i:512 * h2 + 128 * (i + 1)],
                                                pb[:, 128 * (h2 % 4):128 * (h2 % 4) + 128]))
                    if h == 3:
                        em.wait("t", "s", ns)
            self.qT_drain_s = em.n("s")
        self.qT_s = em.n("s")
        em.wait("t", "s", self.qT_s)
        nd = self.load_wslot0(self.wvt)
        em.wait("t", "dwa", nd)
        vcmdrain = {0: 0, 1: 0}
        for oc in range(8):
            pa = self.psA[oc % 2]
            em.wait("t", "v", vcmdrain[oc % 2])
            for cc in range(8):
                em.fin("t", t.matmul(
                    pa[:, 0:512],
                    self.wa[:, 1024 * cc + 128 * oc:1024 * cc + 128 * (oc + 1)],
                    self.xn_d[cc][:, SC * b:SC * (b + 1)],
                    start=(cc == 0), stop=(cc == 7)))
            nt_core = em.n("t")
            em.wait("t", "v", getattr(self, "hal_drain_v", 0))
            for cc in range(8):
                em.fin("t", t.matmul(
                    self.psC[2][:, 0:4],
                    self.wa[:, 1024 * cc + 128 * oc:1024 * cc + 128 * (oc + 1)],
                    self.xnh[:, 32 * cc + 4 * b:32 * cc + 4 * (b + 1)],
                    start=(cc == 0), stop=(cc == 7)))
            nt_halo = em.n("t")
            em.wait("v", "t", nt_halo)
            em.fin("v", v.scalar_tensor_tensor(
                self.vcm_b[:, 516 * oc + 2:516 * oc + 514], pa[:, 0:512], 1.0,
                self.ir0x[:, 0:512], ALU.mult, ALU.mult))
            em.fin("v", v.scalar_tensor_tensor(
                self.vcm_b[:, 516 * oc:516 * oc + 2], self.psC[2][:, 0:2], 1.0,
                self.ir0x[:, 512:514], ALU.mult, ALU.mult))
            nv = em.fin("v", v.scalar_tensor_tensor(
                self.vcm_b[:, 516 * oc + 514:516 * oc + 516], self.psC[2][:, 2:4], 1.0,
                self.ir0x[:, 514:516], ALU.mult, ALU.mult))
            vcmdrain[oc % 2] = nv
            self.hal_drain_v = nv
        nd = self.load_wslot0(self.wgt)
        em.wait("t", "dwa", nd)
        em.wait("t", "v", em.n("v"))
        gdrain = {0: 0, 1: 0}
        for oc in range(8):
            pa = self.psA[oc % 2]
            em.wait("t", "s", gdrain[oc % 2])
            for cc in range(8):
                em.fin("t", t.matmul(
                    pa[:, 0:512],
                    self.wa[:, 1024 * cc + 128 * oc:1024 * cc + 128 * (oc + 1)],
                    self.xn_d[cc][:, SC * b:SC * (b + 1)],
                    start=(cc == 0), stop=(cc == 7)))
            nt = em.n("t")
            em.wait("s", "t", nt)
            gdrain[oc % 2] = em.fin("s", s.copy(self.g_b[:, 512 * oc:512 * (oc + 1)], pa[:, 0:512]))
        em.wait("t", "dkb", self.kvbf_dg)
        em.wait("v", "s", em.n("s"))
        for h in range(8):
            em.wait("v", "v")
            em.fin("v", v.tensor_scalar(
                self.leph[:], self.vcm_b[:, 516 * h:516 * h + 512],
                self.lepw_sb[:, 5 * h:5 * h + 1], self.lepb_sb[:, h:h + 1],
                ALU.mult, ALU.add))
            for j in range(1, 5):
                em.wait("v", "v")
                em.fin("v", v.scalar_tensor_tensor(
                    self.leph[:], self.vcm_b[:, 516 * h + j:516 * h + j + 512],
                    self.lepw_sb[:, 5 * h + j:5 * h + j + 1],
                    self.leph[:], ALU.mult, ALU.add))
            lep_v = em.n("v")
            for i in range(NTB):
                pc = self.psC[2 + (i % 2)]
                em.wait("t", "v", em.n("v"))
                nt = em.fin("t", t.matmul(pc[:, 0:129],
                                          self.qT_b[:, 512 * h + 128 * i:512 * h + 128 * (i + 1)],
                                          self.kv_bf_b[:, 129 * h:129 * (h + 1)],
                                          start=True, stop=True))
                em.wait("v", "t", nt)
                em.fin("v", v.scalar_tensor_tensor(self.t1scr[:, 0:129], pc[:, 0:129], 1.0,
                                                   self.vsumb[:, 129 * h:129 * (h + 1)],
                                                   ALU.mult, ALU.add))
                self.num_read_v = em.n("v")
                em.wait("v", "v")
                em.fin("v", v.reciprocal(self.rden[:], self.t1scr[:, 128:129]))
                em.wait("v", "v")
                nv = em.fin("v", v.tensor_scalar(self.mala_tok[:], self.t1scr[:, 0:128],
                                                 self.rden[:, 0:1], None, ALU.mult))
                em.wait("t", "v", nv)
                pb = self.psB[i % 2]
                nt = em.fin("t", t.transpose(pb[:, 0:128], self.mala_tok[:], self.idb_sb[:]))
                em.wait("v", "t", nt)
                em.wait("v", "v", lep_v)
                em.fin("v", v.scalar_tensor_tensor(self.oscr[:], pb[:, 0:128], 1.0,
                                                   self.leph[:, 128 * i:128 * (i + 1)],
                                                   ALU.mult, ALU.add))
                em.wait("v", "v")
                nv = em.fin("v", v.tensor_tensor(self.o_b[:, 512 * h + 128 * i:512 * h + 128 * (i + 1)],
                                                 self.oscr[:], self.g_b[:, 512 * h + 128 * i:512 * h + 128 * (i + 1)],
                                                 op=ALU.mult))
                em.wait("t", "v", nv)
        self.vsumb_last_read_v = em.n("v")
        self.kvbf_last_read_t = em.n("t")
        nd = self.load_wslot0(self.wot)
        em.wait("t", "dwa", nd)
        em.wait("t", "v", em.n("v"))
        em.wait("v", "t", em.n("t"))
        s0drain = {0: 0, 1: 0}
        for oc in range(8):
            pa = self.psA[oc % 2]
            em.wait("t", "v", s0drain[oc % 2])
            for cc in range(8):
                em.fin("t", t.matmul(
                    pa[:, 0:512],
                    self.wa[:, 1024 * cc + 128 * oc:1024 * cc + 128 * (oc + 1)],
                    self.o_b[:, 512 * cc:512 * (cc + 1)],
                    start=(cc == 0), stop=(cc == 7)))
            nt = em.n("t")
            em.wait("v", "t", nt)
            s0drain[oc % 2] = em.fin("v", v.scalar_tensor_tensor(
                self.s0_b[:, 512 * oc:512 * (oc + 1)],
                pa[:, 0:512], 1.0, self.r0x[:], ALU.mult, ALU.mult))

    def pass2_ffn(self, b):
        em, nc = self.em, self.nc
        v, s, y, t, g = nc.vector, nc.scalar, nc.sync, nc.tensor, nc.gpsimd
        em.wait("t", "v", em.n("v"))
        em.wait("s", "v", em.n("v"))
        em.wait("s", "t", em.n("t"))
        geldrain = {0: 0, 1: 0}
        h2gps = {}
        for fc in range(FF // 128):
            p = fc % 2
            em.wait("y", "t", getattr(self, f"wstr_read_{p}", 0))
            src = self.wup_tern[fc, :, :].rearrange("(cc p) f -> p cc f", p=128)
            nd = em.fin(f"du{p}", y.dma_start(self.wstream[p][:].rearrange("p (cc f) -> p cc f", f=128), src), 16)
            em.wait("t", f"du{p}", nd)
            em.wait("t", "s", geldrain[p])
            for cc in range(8):
                em.fin("t", t.matmul(self.psA[p][:, 0:512],
                                     self.wstream[p][:, 128 * cc:128 * (cc + 1)],
                                     self.xn_d[cc][:, SC * b:SC * (b + 1)],
                                     start=(cc == 0), stop=(cc == 7)))
            setattr(self, f"wstr_read_{p}", em.n("t"))
            if fc > 0:
                em.wait("t", "g", h2gps[fc - 1])
                em.fin("t", t.matmul(self.psC[2][0:1, 0:512], self.onescol_sb[:], self.h2scr[(fc - 1) % 2][:],
                                     start=(fc - 1 == 0), stop=False))
            nt = em.n("t")
            em.wait("s", "t", nt)
            em.wait("s", "g", getattr(self, f"h2_gps_{p}", 0))
            ns = em.fin("s", s.activation(self.h_sb[fc], self.psA[p][:, 0:512], AF.Gelu))
            geldrain[p] = ns
            em.wait("g", "s", ns)
            em.wait("g", "t", em.n("t"))
            ng = em.fin("g", g.tensor_tensor(self.h2scr[p][:], self.h_sb[fc], self.h_sb[fc], op=ALU.mult))
            h2gps[fc] = ng
            setattr(self, f"h2_gps_{p}", ng)
        lastfc = FF // 128 - 1
        em.wait("t", "g", h2gps[lastfc])
        em.fin("t", t.matmul(self.psC[2][0:1, 0:512], self.onescol_sb[:], self.h2scr[lastfc % 2][:],
                             start=False, stop=True))
        em.wait("v", "t", em.n("t"))
        em.fin("v", v.tensor_copy(self.sqh[0:1, :], self.psC[2][0:1, 0:512]))
        em.wait("s", "v", em.n("v"))
        ns = em.fin("s", s.activation(self.sqh[0:1, :], self.sqh[0:1, :], AF.Sqrt, scale=1.0 / FF))
        em.wait("s", "s", ns)
        ns = em.fin("s", s.activation(self.sqh[0:1, :], self.sqh[0:1, :], AF.Identity, bias=self.cst_sb[0:1, 1:2]))
        em.wait("v", "s", ns)
        em.fin("v", v.reciprocal(self.sqh[0:1, :], self.sqh[0:1, :]))
        em.wait("v", "v")
        nv = em.fin("v", v.tensor_scalar(self.r1row_bf[0:1, :], self.sqh[0:1, :], 1.0, None, ALU.mult))
        em.wait("t", "v", nv)
        nt = em.fin("t", t.matmul(self.psC[3][:, 0:512], self.onesrow_sb[:], self.r1row_bf[0:1, :],
                                  start=True, stop=True))
        em.wait("v", "t", nt)
        nv = em.fin("v", v.tensor_copy(self.r1x[:], self.psC[3][:, 0:512]))
        g0drain = {0: 0, 1: 0}
        for oc in range(8):
            pa = self.psA[oc % 2]
            em.wait("t", "s", g0drain[oc % 2])
            for fgrp in range(8):
                p = fgrp % 2
                em.wait("y", "t", getattr(self, f"wds_read_{p}", 0))
                src = self.wdn_tern[oc, 512 * fgrp:512 * (fgrp + 1), :].rearrange(
                    "(fc p) o -> p fc o", p=128)
                nd = em.fin(f"dd{p}", y.dma_start(self.wds[p][:].rearrange("p (fc o) -> p fc o", o=128), src), 16)
                em.wait("t", f"dd{p}", nd)
                for j in range(4):
                    fc = 4 * fgrp + j
                    em.fin("t", t.matmul(pa[:, 0:512],
                                         self.wds[p][:, 128 * j:128 * (j + 1)],
                                         self.h_sb[fc],
                                         start=(fc == 0), stop=(fc == 31)))
                setattr(self, f"wds_read_{p}", em.n("t"))
            nt = em.n("t")
            em.wait("v", "t", nt)
            em.wait("v", "v", em.n("v"))
            em.wait("v", "s", em.n("s"))
            nvd = em.fin("v", v.tensor_tensor(self.dscr[:], pa[:, 0:512], self.r1x[:], op=ALU.mult))
            em.wait("s", "v", nvd)
            g0drain[oc % 2] = em.fin("s", s.activation(self.g0_b[:, 512 * oc:512 * (oc + 1)],
                                                       self.dscr[:], AF.Silu))
        self.g0_drain_s = em.n("s")
        em.wait("t", "s", self.g0_drain_s) if False else None
        self.ffn_s = em.n("s")

    def pass2_x1(self, b):
        em, nc = self.em, self.nc
        v, s, y, t = nc.vector, nc.scalar, nc.sync, nc.tensor
        NTB = self.NTB
        em.wait("t", "s", self.ffn_s)
        for i in range(NTB):
            ti = b * NTB + i
            pb = self.psB[i % 2]
            em.wait("t", "v", getattr(self, "x1_v", 0))
            for src, dst in ((self.s0_b, self.stok), (self.g0_b, self.gtok)):
                for oc in range(8):
                    nt = em.fin("t", t.transpose(pb[:, 128 * (oc % 4):128 * (oc % 4) + 128],
                                                 src[:, 512 * oc + 128 * i:512 * oc + 128 * (i + 1)],
                                                 self.idb_sb[:]))
                    if oc % 4 == 3:
                        em.wait("s", "t", nt)
                        for o2 in range(oc - 3, oc + 1):
                            ns = em.fin("s", s.copy(dst[:, 128 * o2:128 * (o2 + 1)],
                                                    pb[:, 128 * (o2 % 4):128 * (o2 % 4) + 128]))
                        em.wait("t", "s", ns)
            p = i % 2
            war = self.ln_war.get(p)
            if war is not None:
                em.wait("y", "v", war["v"])
            nd = em.fin(f"dx{p}", y.dma_start(self.xld[p][:], self.x0[128 * ti:128 * (ti + 1), :]), 16)
            em.wait("v", "s", em.n("s"))
            em.fin("v", v.tensor_tensor(self.sgt[:], self.stok[:], self.gtok[:], op=ALU.mult))
            em.wait("v", f"dx{p}", nd)
            em.wait("v", "v")
            nv = em.fin("v", v.tensor_tensor(self.x1t[:], self.xld[p][:], self.sgt[:], op=ALU.add))
            self.x1_v = nv
            self.ln_war[p] = {"s": em.n("s"), "v": nv}
            em.wait("y", "v", nv)
            em.fin("dxo", y.dma_start(self.x1o[128 * ti:128 * (ti + 1), :], self.x1t[:]), 16)
            if "s0" in self.dbg:
                em.wait("g", "s", em.n("s"))
                em.fin("ddbg", nc.gpsimd.dma_start(self.s0_dbg[128 * ti:128 * (ti + 1), :], self.stok[:]), 16)
            if "g0" in self.dbg:
                em.wait("g", "s", em.n("s"))
                em.fin("ddbg", nc.gpsimd.dma_start(self.g0_dbg[128 * ti:128 * (ti + 1), :], self.gtok[:]), 16)
            em.wait("s", "dxo", em.n("dxo"))
            em.wait("v", "dxo", em.n("dxo"))
            if "s0" in self.dbg or "g0" in self.dbg:
                em.wait("s", "ddbg", em.n("ddbg"))



def l0_host_inputs(core, x_full, Wq, Wk, Wv, Wg, Wo, w_up, w_down, lepe_w, lepe_b,
                   NB=B):
    Sx = x_full.shape[1]
    s0, s1 = core * SC, (core + 1) * SC
    xs = np.ascontiguousarray(x_full[:NB, s0:s1, :].reshape(NB * SC, D), dtype=np.float32)
    xh = np.zeros((32, D), np.float32)
    for b in range(NB):
        if s0 >= 2:
            xh[4 * b + 0] = x_full[b, s0 - 2]
            xh[4 * b + 1] = x_full[b, s0 - 1]
        elif s0 == 1:
            xh[4 * b + 1] = x_full[b, 0]
        if s1 <= Sx - 2:
            xh[4 * b + 2] = x_full[b, s1]
            xh[4 * b + 3] = x_full[b, s1 + 1]
        elif s1 == Sx - 1:
            xh[4 * b + 2] = x_full[b, s1]
    cos, sgn = rope_tables(np.arange(s0, s1))
    T = lambda a: np.ascontiguousarray(np.asarray(a, np.float32).T)
    lw = np.ascontiguousarray(np.asarray(lepe_w, np.float32)[:, 0, :]
                              .reshape(8, 128, 5).transpose(1, 0, 2).reshape(128, 40))
    lb = np.ascontiguousarray(np.asarray(lepe_b, np.float32).reshape(8, 128).T)
    return {
        "x0": xs, "xh": xh,
        "wqt": T(Wq), "wkt": T(Wk), "wvt": T(Wv), "wgt": T(Wg), "wot": T(Wo),
        "wupT": T(w_up), "wdnT": T(w_down),
        "lepw": lw, "lepb": lb,
        "costab": cos, "sgntab": sgn,
        "idb": np.eye(128, dtype=ml_dtypes.bfloat16),
        "idf": np.eye(128, dtype=np.float32),
        "onesrow": np.ones((1, 128), ml_dtypes.bfloat16),
        "onescol": np.ones((128, 1), ml_dtypes.bfloat16),
        "onesrow_f": np.ones((1, 128), np.float32),
        "cst": np.tile(np.array([[LN_EPS, RMS_EPS, 0.0, 1.0]], np.float32), (128, 1)),
    }


def build_l0_nc(n_cores=N_CORES, NB=B, stop_after=None, dbg=()):
    from contextlib import ExitStack
    nc = bass.Bass("TRN2", target_bir_lowering=False)
    ctx = ExitStack()
    with ctx:
        em = Emit(nc, ctx)
        l0 = L0(nc, em, ctx, n_cores=n_cores, NB=NB, stop_after=stop_after, dbg=dbg)
        l0.emit()
    return nc




NH = 2
HP = H1 // NH
W68 = E1 + 4
W196 = 3 * E1 + 4


class L1:
    def __init__(self, nc, em, ctx, n_cores=N_CORES, NL=B, dbg=(), prefix="l1_",
                 stop_after=None, x1_ap=None):
        self.nc, self.em, self.ctx = nc, em, ctx
        self.stop_after = stop_after
        self.x1_ap = x1_ap
        self.n_cores, self.NL = n_cores, NL
        self.dbg = set(dbg)
        self.px = prefix
        self.TPCL = NL * SC
        self.NTL = self.TPCL // 128
        self.NNT = SC // 128
        self.lnp = 0
        self.ln_war = {}

    def declare_io(self):
        nc, NL, px = self.nc, self.NL, self.px
        g = lambda n, sh, dt=F32, kind="ExternalInput": nc.dram_tensor(px + n, sh, dt, kind=kind).ap()
        self.x1 = self.x1_ap if self.x1_ap is not None else g("x1", [NL * SC, D])
        self.wy0b = g("wy0b", [128, H1 * W68])
        self.wcat = g("wcat", [128, H1 * W196])
        self.wq0T = g("wq0T", [128, H1 * E1])
        self.wk0T = g("wk0T", [128, H1 * E1])
        self.woutT = g("woutT", [D, D])
        self.gamma = g("gamma", [1, D])
        self.beta = g("beta", [1, D])
        self.gatew = g("gatew", [128, 8 * EXP])
        self.swg = g("swg", [D, FM]); self.swu = g("swu", [D, FM]); self.swd = g("swd", [FM, D])
        self.ewg = g("ewg", [EXP, D, FM]); self.ewu = g("ewu", [EXP, D, FM]); self.ewd = g("ewd", [EXP, FM, D])
        self.idb = g("idb", [128, 128], BF16)
        self.onesrow = g("onesrow", [1, 128], BF16)
        self.onescol = g("onescol", [128, 1], BF16)
        self.cst = g("cst", [128, 4], F32)
        self.outo = g("outo", [NL * SC, D], F32, kind="ExternalOutput")
        self.su_hbm = nc.dram_tensor(px + "su_hbm", [FM, self.TPCL], BF16).ap()
        self.ewbf_g = nc.dram_tensor(px + "ewbf_g", [EXP + 1, D, FM], BF16).ap()
        self.ewbf_u = nc.dram_tensor(px + "ewbf_u", [EXP + 1, D, FM], BF16).ap()
        self.ewbf_d = nc.dram_tensor(px + "ewbf_d", [EXP + 1, FM, D], BF16).ap()
        self.routed_hbm = nc.dram_tensor(px + "routed_hbm", [D, self.TPCL], F32).ap()
        self.w_hbm = nc.dram_tensor(px + "w_hbm", [EXP + 1, self.TPCL], BF16).ap()
        self.g1_hbm = nc.dram_tensor(px + "g1_hbm", [NL * SC, D], BF16).ap()
        self.s1p_hbm = nc.dram_tensor(px + "s1p_hbm", [NL * SC, D], BF16).ap()

    def alloc_static(self):
        nc, ctx, px = self.nc, self.ctx, self.px
        sb = lambda name, shape, dt: ctx.enter_context(nc.sbuf_tensor(px + name, shape, dt))
        self.sbf = sb
        self.xn_d = [sb(f"xn2_{cc}", [128, self.TPCL], BF16) for cc in range(8)]
        self.xld = [sb(f"xld{i}", [128, D], F32) for i in range(2)]
        self.stats = sb("stats", [128, 8], F32)
        self.statsall = sb("statsall", [128, 2 * NT], F32)
        self.tiny = sb("tiny", [128, 64], F32)
        self.act_scr = sb("act_scr", [128, D], BF16)
        self.act_scr2 = sb("act_scr2", [128, D], BF16)
        self.idb_sb = sb("idb_sb", [128, 128], BF16)
        self.onesrow_sb = sb("onesrow_sb", [1, 128], BF16)
        self.onescol_sb = sb("onescol_sb", [128, 1], BF16)
        self.cst_sb = sb("cst_sb", [128, 4], F32)
        self.wy0b_sb = sb("wy0b_sb", [128, H1 * W68], BF16)
        self.wcat_sb = sb("wcat_sb", [128, H1 * W196], BF16)
        self.gatew_sb = sb("gatew_sb", [128, 8 * EXP], BF16)
        self.gx = sb("gx", [128, D], BF16)
        self.bx = sb("bx", [128, D], BF16)
        self.grow = sb("grow", [1, D], BF16)
        self.sq0x = sb("sq0x", [128, H1 * E1], BF16)
        self.sk0x = sb("sk0x", [128, H1 * E1], BF16)
        self.srow = sb("srow", [1, 2 * E1], BF16)
        self.psA = [ctx.enter_context(nc.psum_tensor(px + f"psA{i}", [128, 512], F32)) for i in range(2)]
        self.psB = [ctx.enter_context(nc.psum_tensor(px + f"psB{i}", [128, 512], BF16)) for i in range(2)]
        self.psC = [ctx.enter_context(nc.psum_tensor(px + f"psC{i}", [128, 512], F32)) for i in range(4)]

    def emit(self):
        from contextlib import ExitStack
        self.declare_io()
        self.alloc_static()
        em = self.em
        with ExitStack() as pp:
            sbp = lambda name, shape, dt: pp.enter_context(self.nc.sbuf_tensor(self.px + name, shape, dt))
            self.wq0T_sb = sbp("wq0T_sb", [128, H1 * E1], BF16)
            self.wk0T_sb = sbp("wk0T_sb", [128, H1 * E1], BF16)
            self.xn_stage = [sbp(f"xnst{i}", [128, D], BF16) for i in range(2)]
            self.preload()
            self.ln2_pass()
            em.barrier()
        with ExitStack() as pg:
            sb = lambda name, shape, dt: pg.enter_context(self.nc.sbuf_tensor(self.px + name, shape, dt))
            self.gstage = sb("gstage", [128, 64], F32)
            self.gmisc = sb("gmisc", [128, 16], F32)
            self.wstage = sb("wstage", [1, (EXP + 1) * 128], BF16)
            self.wbf = sb("wbf", [128, EXP], BF16)
            self.growst = sb("growst", [8, 512], BF16)
            self.moe_gating()
            self.moe_precast()
            em.barrier()
        with ExitStack() as pm:
            sb = lambda name, shape, dt: pm.enter_context(self.nc.sbuf_tensor(self.px + name, shape, dt))
            self.wrowld = sb("wrowld", [1, 512], BF16)
            self.wexp = sb("wexp", [128, 8 * 512], BF16)
            self.webuf = [sb(f"webuf{i}", [128, 1024], BF16) for i in range(4)]
            self.egbuf = sb("egbuf", [128, 512], BF16)
            self.gcpbuf = sb("gcpbuf", [128, 512], BF16)
            self.tg2buf = sb("tg2buf", [128, 512], BF16)
            self.sgbuf = sb("sgbuf", [128, 512], BF16)
            self.sustage = sb("sustage", [128, 512], BF16)
            self.sufin = [sb(f"sufin{i}", [128, 512], BF16) for i in range(2)]
            self.dnbuf = [sb(f"dnbuf{i}", [128, 1024], BF16) for i in range(2)]
            self.su_tg = sb("su_tg", [128, 8 * 512], BF16)
            self.rstage = sb("rstage", [128, 512], F32)
            self.woutsb = sb("woutsb", [128, 8 * 1024], BF16)
            self.Kbuf = sb("Kbuf", [128, self.NL * NH * E1], BF16)
            self.Ubuf = sb("Ubuf", [128, NH * 3 * E1 * self.NL], BF16)
            self.Ubbuf = sb("Ubbuf", [128, NH * 4 * self.NL], BF16)
            self.sqskst = sb("sqskst", [128, 2 * NH * E1], F32)
            self.qkbuf = sb("qkbuf", [128, 2 * NH * E1], F32)
            self.xcur_all = sb("xcur_all", [128, self.NL * 128], BF16)
            self.y0ball = sb("y0ball", [128, self.NL * NH * W68], BF16)
            self.dcur = sb("dcur", [128, NH * E1], F32)
            self.dT = sb("dT", [128, 128], BF16)
            self.dTe = sb("dTe", [128, 128], BF16)
            self.adots = sb("adots", [128, self.NL * NH], F32)
            self.bdots = sb("bdots", [128, self.NL * NH], F32)
            self.bexp3 = sb("bexp3", [128, NH * 3 * self.NL], F32)
            self.scr1 = sb("scr1", [128, NH * 3 * E1 * self.NL], F32)
            self.ycorr = sb("ycorr", [128, NH * E1], F32)
            self.bcorr = sb("bcorr", [128, NH * 4], F32)
            self.corr3 = sb("corr3", [128, NH * 3 * E1], F32)
            self.diffs = sb("diffs", [128, NH * W196], F32)
            self.beta_t = sb("beta_t", [128, NH * 4], F32)
            self.beta_s = sb("beta_s", [128, NH * 4], F32)
            self.segred = sb("segred", [128, 2 * NH], F32)
            self.yt = sb("yt", [128, NH * E1], BF16)
            self.Ytile = [sb(f"Yt{cc}", [128, 8 * 128], BF16) for cc in range(8)]
            self.s1pre = sb("s1pre", [128, 8 * 512], BF16)
            self.s1tile = sb("s1tile", [128, D], BF16)
            moe_gen = self.moe_expert_units()
            self.moe_alive = True
            for nt_i in range(self.NNT):
                for hp in range(HP):
                    self.scan_tile(nt_i, hp, moe_gen)
                self.wout_spill(nt_i, moe_gen)
            while self.moe_alive:
                self.qm(moe_gen, force=True)
            em.barrier()
        with ExitStack() as pf:
            sb = lambda name, shape, dt: pf.enter_context(self.nc.sbuf_tensor(self.px + name, shape, dt))
            self.s1ld = [sb(f"s1ld{i}", [128, D], BF16) for i in range(2)]
            self.rld2 = [sb(f"rld2{i}", [128, 512], F32) for i in range(2)]
            self.silu_b = sb("silu_b", [128, 512], BF16)
            self.g1t4 = [sb(f"g1t4_{i}", [128, D], BF16) for i in range(4)]
            self.outt = sb("outt", [128, D], F32)
            self.rstdall = sb("rstdall", [128, NT], F32)
            self.meanall = sb("meanall", [128, NT], F32)
            self.g1_final()
        em.barrier()

    def qm(self, gen, force=False):
        try:
            next(gen)
        except StopIteration:
            self.moe_alive = False

    def preload(self):
        em, nc = self.em, self.nc
        y, g, v, t = nc.sync, nc.gpsimd, nc.vector, nc.tensor
        px = self.px
        dc = px + "dcst"
        em.fin(dc, y.dma_start(self.idb_sb[:], self.idb), 16)
        em.fin(dc, y.dma_start(self.onesrow_sb[:], self.onesrow), 16)
        em.fin(dc, y.dma_start(self.onescol_sb[:], self.onescol), 16)
        em.fin(dc, y.dma_start(self.cst_sb[:], self.cst), 16)
        self.c_dy = em.n(dc)
        dw = px + "dwc"
        em.fin(dw, g.dma_start(self.wy0b_sb[:], self.wy0b), 16)
        em.fin(dw, g.dma_start(self.wcat_sb[:], self.wcat), 16)
        em.fin(dw, g.dma_start(self.wq0T_sb[:], self.wq0T), 16)
        em.fin(dw, g.dma_start(self.wk0T_sb[:], self.wk0T), 16)
        em.fin(dw, g.dma_start(self.gatew_sb[:], self.gatew), 16)
        em.fin(dw, g.dma_start(self.grow[0:1, :], self.gamma[0:1, :]), 16)
        self.wc_dg = em.n(dw)
        em.wait("t", dw, self.wc_dg)
        em.wait("t", dc, self.c_dy)
        nt = em.fin("t", t.matmul(self.psA[0][:, 0:512], self.onesrow_sb[:], self.grow[0:1, 0:512], start=True, stop=True))
        nt = em.fin("t", t.matmul(self.psA[1][:, 0:512], self.onesrow_sb[:], self.grow[0:1, 512:1024], start=True, stop=True))
        em.wait("v", "t", nt)
        em.fin("v", v.tensor_copy(self.gx[:, 0:512], self.psA[0][:, 0:512]))
        nv = em.fin("v", v.tensor_copy(self.gx[:, 512:1024], self.psA[1][:, 0:512]))
        em.wait("g", "v", nv)
        nd = em.fin(dw, g.dma_start(self.grow[0:1, :], self.beta[0:1, :]), 16)
        em.wait("t", dw, nd)
        em.wait("t", "v", nv)
        nt = em.fin("t", t.matmul(self.psA[0][:, 0:512], self.onesrow_sb[:], self.grow[0:1, 0:512], start=True, stop=True))
        nt = em.fin("t", t.matmul(self.psA[1][:, 0:512], self.onesrow_sb[:], self.grow[0:1, 512:1024], start=True, stop=True))
        em.wait("v", "t", nt)
        em.fin("v", v.tensor_copy(self.bx[:, 0:512], self.psA[0][:, 0:512]))
        nv = em.fin("v", v.tensor_copy(self.bx[:, 512:1024], self.psA[1][:, 0:512]))
        for src_sb, dst in ((self.wq0T_sb, self.sq0x), (self.wk0T_sb, self.sk0x)):
            for hp in range(HP):
                em.wait("t", "v", em.n("v"))
                for j in range(NH):
                    h = NH * hp + j
                    base = 64 * j
                    nt = em.fin("t", t.matmul(self.psC[2 + j][0:1, 0:64],
                                              self.onescol_sb[base:base + 64, 0:1],
                                              src_sb[base:base + 64, E1 * h:E1 * (h + 1)],
                                              start=True, stop=True))
                em.wait("v", "t", nt)
                em.fin("v", v.tensor_copy(self.srow[0:1, 0:64], self.psC[2][0:1, 0:64]))
                nv = em.fin("v", v.tensor_copy(self.srow[0:1, 64:128], self.psC[3][0:1, 0:64]))
                em.wait("t", "v", nv)
                nt = em.fin("t", t.matmul(self.psC[0][:, 0:128], self.onesrow_sb[:],
                                          self.srow[0:1, :], start=True, stop=True))
                em.wait("v", "t", nt)
                nv = em.fin("v", v.tensor_copy(dst[:, 128 * hp:128 * (hp + 1)], self.psC[0][:, 0:128]))
        self.pre_v = em.n("v")

    def ln_tile(self, xsrc, p):
        em, nc = self.em, self.nc
        v, s, y = nc.vector, nc.scalar, nc.sync
        px = self.px
        war = self.ln_war.get(p)
        if war is not None:
            em.wait("y", "s", war["s"])
            em.wait("y", "v", war["v"])
        nd = em.fin(px + f"dx{p}", y.dma_start(self.xld[p][:, :], xsrc), 16)
        em.wait("s", px + f"dx{p}", nd)
        st, T = self.stats, self.tiny
        c = 16 * p
        em.fin("s", s.activation(self.act_scr[:, :], self.xld[p][:, :],
                                 AF.Identity, accum_out=st[:, 4 * p:4 * p + 1]))
        ns = em.fin("s", s.activation(self.act_scr2[:, :], self.xld[p][:, :],
                                      AF.Square, accum_out=st[:, 4 * p + 1:4 * p + 2]))
        em.wait("v", "s", ns)
        em.fin("v", v.tensor_scalar(T[:, c:c + 1], st[:, 4 * p:4 * p + 1], 1.0 / D, None, ALU.mult))
        em.fin("v", v.tensor_scalar(T[:, c + 1:c + 2], st[:, 4 * p + 1:4 * p + 2], 1.0 / D, None, ALU.mult))
        em.wait("v", "v")
        em.fin("v", v.tensor_tensor(T[:, c + 2:c + 3], T[:, c:c + 1], T[:, c:c + 1], op=ALU.mult))
        em.wait("v", "v")
        nv = em.fin("v", v.tensor_tensor(T[:, c + 3:c + 4], T[:, c + 1:c + 2], T[:, c + 2:c + 3], op=ALU.subtract))
        em.wait("s", "v", nv)
        ns = em.fin("s", s.activation(T[:, c + 4:c + 5], T[:, c + 3:c + 4], AF.Sqrt, bias=self.cst_sb[:, 0:1]))
        em.wait("v", "s", ns)
        nv = em.fin("v", v.reciprocal(T[:, c + 5:c + 6], T[:, c + 4:c + 5]))
        em.wait("v", "v")
        nv = em.fin("v", v.tensor_scalar(self.xn_stage[p][:, :], self.xld[p][:, :],
                                         T[:, c:c + 1], T[:, c + 5:c + 6], ALU.subtract, ALU.mult))
        self.ln_war[p] = {"s": em.n("s"), "v": nv}
        return nv

    def ln2_pass(self):
        em, nc = self.em, self.nc
        s, t = nc.scalar, nc.tensor
        em.wait("t", self.px + "dcst", self.c_dy)
        for ti in range(self.NTL):
            p = self.lnp
            self.lnp ^= 1
            nv = self.ln_tile(self.x1[128 * ti:128 * (ti + 1), :], p)
            em.wait("t", "v", nv)
            em.wait("t", "s", getattr(self, "xn_drain_s", 0))
            pb = self.psB[ti % 2]
            for cc in range(8):
                nt = em.fin("t", t.transpose(pb[:, 128 * (cc % 4):128 * (cc % 4) + 128],
                                             self.xn_stage[p][:, 128 * cc:128 * (cc + 1)],
                                             self.idb_sb[:]))
                if cc % 4 == 3:
                    em.wait("s", "t", nt)
                    for c2 in range(cc - 3, cc + 1):
                        ns = em.fin("s", s.copy(self.xn_d[c2][:, 128 * ti:128 * (ti + 1)],
                                                pb[:, 128 * (c2 % 4):128 * (c2 % 4) + 128]))
                    if cc == 3:
                        em.wait("t", "s", ns)
            self.xn_drain_s = em.n("s")

    def moe_gating(self):
        em, nc = self.em, self.nc
        v, s, y, t = nc.vector, nc.scalar, nc.sync, nc.tensor
        px = self.px
        em.wait("t", px + "dwc", self.wc_dg)
        for tg in range(self.NL):
            em.wait("t", "v", em.n("v"))
            for cc in range(8):
                em.fin("t", t.matmul(self.psC[0][0:EXP, 0:512],
                                     self.gatew_sb[:, 8 * cc:8 * (cc + 1)],
                                     self.xn_d[cc][:, SC * tg:SC * (tg + 1)],
                                     start=(cc == 0), stop=(cc == 7)))
            nt = em.n("t")
            em.wait("v", "t", nt)
            nv = em.fin("v", v.tensor_copy(self.growst[0:EXP, :], self.psC[0][0:EXP, 0:512]))
            em.wait("t", "v", nv)
            for i in range(4):
                nt = em.fin("t", t.transpose(self.psB[0][:, 8 * i:8 * (i + 1)],
                                             self.growst[0:EXP, 128 * i:128 * (i + 1)],
                                             self.idb_sb[0:EXP, 0:EXP]))
            em.wait("v", "t", em.n("t"))
            for i in range(4):
                ti = 4 * tg + i
                G = self.gstage
                M = self.gmisc
                em.wait("v", "v")
                em.fin("v", v.tensor_copy(G[:, 0:8], self.psB[0][:, 8 * i:8 * (i + 1)]))
                em.wait("v", "v")
                em.fin("v", v.tensor_reduce(M[:, 0:1], G[:, 0:8], axis=AX.X, op=ALU.max, negate=True))
                em.wait("s", "v", em.n("v"))
                ns = em.fin("s", s.activation(G[:, 8:16], G[:, 0:8], AF.Exp, bias=M[:, 0:1],
                                              accum_out=M[:, 1:2]))
                em.wait("v", "s", ns)
                em.fin("v", v.reciprocal(M[:, 2:3], M[:, 1:2]))
                em.wait("v", "v")
                em.fin("v", v.tensor_scalar(G[:, 16:24], G[:, 8:16], M[:, 2:3], None, ALU.mult))
                em.wait("v", "v")
                em.fin("v", v.tensor_reduce(M[:, 3:4], G[:, 16:24], axis=AX.X, op=ALU.max))
                em.wait("v", "v")
                em.fin("v", v.tensor_scalar(G[:, 24:32], G[:, 16:24], M[:, 3:4], None, ALU.is_lt))
                em.wait("v", "v")
                em.fin("v", v.tensor_tensor(G[:, 32:40], G[:, 16:24], G[:, 24:32], op=ALU.mult))
                em.wait("v", "v")
                em.fin("v", v.tensor_reduce(M[:, 4:5], G[:, 32:40], axis=AX.X, op=ALU.max))
                em.wait("v", "v")
                em.fin("v", v.tensor_tensor(M[:, 5:6], M[:, 3:4], M[:, 4:5], op=ALU.add))
                em.fin("v", v.tensor_scalar(G[:, 40:48], G[:, 16:24], M[:, 4:5], None, ALU.is_ge))
                em.wait("v", "v")
                em.fin("v", v.tensor_scalar(M[:, 6:7], M[:, 5:6], 1e-20, None, ALU.add))
                em.fin("v", v.tensor_tensor(G[:, 48:56], G[:, 16:24], G[:, 40:48], op=ALU.mult))
                em.wait("v", "v")
                em.fin("v", v.reciprocal(M[:, 7:8], M[:, 6:7]))
                em.wait("v", "v")
                em.fin("v", v.tensor_scalar(G[:, 56:64], G[:, 48:56], M[:, 7:8], None, ALU.mult))
                em.wait("v", "v")
                nv = em.fin("v", v.tensor_scalar(self.wbf[:], G[:, 56:64], 1.0, None, ALU.mult))
                em.wait("t", "v", nv)
                for e in range(EXP):
                    nt = em.fin("t", t.matmul(self.psC[2][0:1, 128 * (e % 4):128 * (e % 4) + 128],
                                              self.wbf[:, e:e + 1], self.idb_sb[:],
                                              start=True, stop=True))
                    if e % 4 == 3:
                        em.wait("v", "t", nt)
                        for e2 in range(e - 3, e + 1):
                            nv = em.fin("v", v.tensor_copy(
                                self.wstage[0:1, 128 * e2:128 * (e2 + 1)],
                                self.psC[2][0:1, 128 * (e2 % 4):128 * (e2 % 4) + 128]))
                        em.wait("t", "v", nv)
                em.wait("v", "v")
                nv = em.fin("v", v.memset(self.wstage[0:1, 128 * EXP:128 * (EXP + 1)], 1.0))
                em.wait("y", "v", nv)
                nd = em.fin(px + "dwh", y.dma_start(
                    self.w_hbm[:, 128 * ti:128 * (ti + 1)].unsqueeze(0),
                    self.wstage[0:1, :].rearrange("o (e t) -> o e t", t=128)), 16)
                em.wait("v", px + "dwh", nd)
        self.gating_done = em.n(px + "dwh")

    def moe_precast(self):
        em, nc = self.em, self.nc
        g = nc.gpsimd
        px = self.px
        for e in range(EXP + 1):
            shared = (e == EXP)
            for dst, srcw in ((self.ewbf_g, self.swg if shared else self.ewg[e, :, :]),
                              (self.ewbf_u, self.swu if shared else self.ewu[e, :, :]),
                              (self.ewbf_d, self.swd if shared else self.ewd[e, :, :])):
                em.fin(px + "dpc", g.dma_start(dst[e, :, :], srcw), 16)
        self.precast_d = em.n(px + "dpc")

    def moe_expert_units(self):
        em, nc = self.em, self.nc
        v, s, y, t, g = nc.vector, nc.scalar, nc.sync, nc.tensor, nc.gpsimd
        px = self.px
        NL = self.NL
        em.wait("y", px + "dpc", self.precast_d)
        for e in range(EXP + 1):
            wg_src = self.ewbf_g[e, :, :]
            wu_src = self.ewbf_u[e, :, :]
            wd_src = self.ewbf_d[e, :, :]
            for fc in range(8):
                pw = fc % 2
                for kk, wsrc in ((0, wg_src), (1, wu_src)):
                    em.wait("y", "tM", getattr(self, f"webuf_rd_{pw}", 0))
                    src = wsrc[:, 128 * fc:128 * (fc + 1)].rearrange("(cc p) f -> p cc f", p=128)
                    nd = em.fin(px + f"dwe{pw}{kk}", y.dma_start(
                        self.webuf[2 * pw + kk][:].rearrange("p (cc f) -> p cc f", f=128), src), 16)
                    setattr(self, f"webuf_ld_{pw}_{kk}", nd)
                for tg in range(NL):
                    upar = (fc * NL + tg) % 2
                    bg = self.psA[0] if upar == 0 else self.psC[0]
                    bu = self.psA[1] if upar == 0 else self.psC[1]
                    if fc == 0:
                        em.wait("y", "tM", getattr(self, "wrow_rd_t", 0))
                        em.wait("y", px + "dwh", self.gating_done)
                        nd = em.fin(px + "dwl", y.dma_start(
                            self.wrowld[0:1, 0:512],
                            self.w_hbm[e:e + 1, SC * tg:SC * (tg + 1)]), 16)
                        em.wait("t", px + "dwl", nd)
                        em.wait("t", "vM", getattr(self, f"su_drain_{upar}", 0))
                        em.wait("t", "vM", getattr(self, "wexp_rd_v", 0))
                        nt = em.fin("tM", t.matmul(bg[:, 0:512], self.onesrow_sb[:],
                                                   self.wrowld[0:1, 0:512], start=True, stop=True))
                        self.wrow_rd_t = nt
                        em.wait("v", "tM", nt)
                        nv = em.fin("vM", v.tensor_copy(self.wexp[:, 512 * tg:512 * (tg + 1)],
                                                        bg[:, 0:512]))
                        self.wexp_rd_v = nv
                        em.wait("t", "vM", nv)
                    em.wait("t", px + f"dwe{pw}0", getattr(self, f"webuf_ld_{pw}_0"))
                    em.wait("t", px + f"dwe{pw}1", getattr(self, f"webuf_ld_{pw}_1"))
                    em.wait("t", "vM", getattr(self, f"su_drain_{upar}", 0))
                    for cc in range(8):
                        em.fin("tM", t.matmul(bg[:, 0:512],
                                              self.webuf[2 * pw][:, 128 * cc:128 * (cc + 1)],
                                              self.xn_d[cc][:, SC * tg:SC * (tg + 1)],
                                              start=(cc == 0), stop=(cc == 7)))
                    yield
                    for cc in range(8):
                        em.fin("tM", t.matmul(bu[:, 0:512],
                                              self.webuf[2 * pw + 1][:, 128 * cc:128 * (cc + 1)],
                                              self.xn_d[cc][:, SC * tg:SC * (tg + 1)],
                                              start=(cc == 0), stop=(cc == 7)))
                    nt = em.n("tM")
                    setattr(self, f"webuf_rd_{pw}", nt)
                    em.wait("v", "tM", nt)
                    em.wait("v", "gM", getattr(self, "su_gM", 0))
                    em.wait("v", "vM", getattr(self, "su_fin_v", 0))
                    nvc = em.fin("vM", v.tensor_copy(self.gcpbuf[:], bg[:, 0:512]))
                    nv = em.fin("vM", v.scalar_tensor_tensor(self.sustage[:], bu[:, 0:512], 1.0,
                                                             self.wexp[:, 512 * tg:512 * (tg + 1)],
                                                             ALU.mult, ALU.mult))
                    setattr(self, f"su_drain_{upar}", nv)
                    self.su_sM = nv
                    self.su_vM = nv
                    em.wait("s", "vM", nvc)
                    ns = em.fin("sM", s.activation(self.egbuf[:], self.gcpbuf[:], AF.Tanh, scale=0.5))
                    em.wait("g", "sM", ns)
                    em.wait("g", "gM", em.n("gM"))
                    em.fin("gM", g.tensor_scalar(self.tg2buf[:], self.egbuf[:], 0.5, 0.5, ALU.mult, ALU.add))
                    em.wait("g", "gM", em.n("gM"))
                    ng = em.fin("gM", g.tensor_tensor(self.sgbuf[:], self.gcpbuf[:], self.tg2buf[:], op=ALU.mult))
                    self.su_gM = ng
                    pf = tg % 2
                    em.wait("v", "gM", ng)
                    em.wait("v", px + "dsu", em.n(px + "dsu"))
                    em.wait("v", "vM", nv)
                    nv = em.fin("vM", v.tensor_tensor(self.sufin[pf][:], self.sustage[:], self.sgbuf[:], op=ALU.mult))
                    self.su_fin_v = nv
                    em.wait("y", "vM", nv)
                    nd = em.fin(px + "dsu", y.dma_start(
                        self.su_hbm[128 * fc:128 * (fc + 1), SC * tg:SC * (tg + 1)],
                        self.sufin[pf][:]), 16)
                    setattr(self, f"su_st_{pf}", nd)
                    yield
            em.wait("y", px + "dsu", em.n(px + "dsu"))
            for tg in range(NL):
                em.wait("y", "tM", getattr(self, "sutg_rd", 0))
                nd = em.fin(px + "dsl", y.dma_start(
                    self.su_tg[:].rearrange("p (fc t) -> p fc t", t=512),
                    self.su_hbm[:, SC * tg:SC * (tg + 1)].rearrange("(fc p) t -> p fc t", p=128)), 16)
                self.sutg_ld = nd
                for oc in range(8):
                    pw = oc % 2
                    bk = [self.psA[0], self.psA[1], self.psC[0], self.psC[1]][oc % 4]
                    em.wait("y", "tM", getattr(self, f"dn_rd_{pw}", 0))
                    src = wd_src[:, 128 * oc:128 * (oc + 1)].rearrange("(fc p) o -> p fc o", p=128)
                    nd = em.fin(px + f"ddn{pw}", y.dma_start(
                        self.dnbuf[pw][:].rearrange("p (fc o) -> p fc o", o=128), src), 16)
                    em.wait("t", px + f"ddn{pw}", nd)
                    em.wait("t", px + "dsl", self.sutg_ld)
                    em.wait("t", "vM", getattr(self, f"dnb_drain_{oc % 4}", 0))
                    for fc in range(8):
                        em.fin("tM", t.matmul(bk[:, 0:512],
                                              self.dnbuf[pw][:, 128 * fc:128 * (fc + 1)],
                                              self.su_tg[:, 512 * fc:512 * (fc + 1)],
                                              start=(fc == 0), stop=(fc == 7)))
                    nt = em.n("tM")
                    setattr(self, f"dn_rd_{pw}", nt)
                    em.wait("v", "tM", nt)
                    em.wait("v", px + "drt", em.n(px + "drt"))
                    em.wait("v", "vM", getattr(self, "rst_v", 0))
                    nv = em.fin("vM", v.tensor_copy(self.rstage[:, 0:512],
                                                    bk[:, 0:512]))
                    self.rst_v = nv
                    setattr(self, f"dnb_drain_{oc % 4}", nv)
                    em.wait("g", "vM", nv)
                    nd = em.fin(px + "drt", g.dma_start(
                        self.routed_hbm[128 * oc:128 * (oc + 1), SC * tg:SC * (tg + 1)],
                        self.rstage[:, 0:512],
                        accum_op=(ALU.bypass if e == 0 else ALU.add)), 16)
                    yield
                self.sutg_rd = em.n("tM")
        self.moe_done_dma = em.n(px + "drt")

    def scan_tile(self, nt_i, hp, gen):
        em, nc = self.em, self.nc
        v, s, t = nc.vector, nc.scalar, nc.tensor
        NL = self.NL
        if nt_i == 0 and hp == 0:
            for cc in range(8):
                nd = em.fin(self.px + "dwo", nc.gpsimd.dma_start(
                    self.woutsb[:, 1024 * cc:1024 * (cc + 1)],
                    self.woutT[128 * cc:128 * (cc + 1), :]), 16)
            self.wout_ld = em.n(self.px + "dwo")
        em.wait("t", "sS", em.n("sS"))
        em.wait("t", "vS", em.n("vS"))
        for lt in range(NL):
            tok0 = SC * lt + 128 * nt_i
            nt = em.fin("tS", t.transpose(self.psB[0][:, 128 * (lt % 4):128 * (lt % 4) + 128],
                                          self.xn_d[hp][:, tok0:tok0 + 128], self.idb_sb[:]))
            if lt % 4 == 3 or lt == NL - 1:
                em.wait("s", "tS", nt)
                for l2 in range(4 * (lt // 4), lt + 1):
                    ns = em.fin("sS", s.copy(self.xcur_all[:, 128 * l2:128 * (l2 + 1)],
                                             self.psB[0][:, 128 * (l2 % 4):128 * (l2 % 4) + 128]))
                em.wait("t", "sS", ns)
        for lt in range(NL):
            tok0 = SC * lt + 128 * nt_i
            for j in range(NH):
                h = NH * hp + j
                base = 64 * j
                em.fin("tS", t.matmul(self.psC[2 + j][:, 0:W68],
                                      self.xn_d[hp][base:base + 64, tok0:tok0 + 128],
                                      self.wy0b_sb[base:base + 64, W68 * h:W68 * (h + 1)],
                                      start=True, stop=True))
            nt = em.n("tS")
            em.wait("s", "tS", nt)
            for j in range(NH):
                ns = em.fin("sS", s.copy(self.y0ball[:, (lt * NH + j) * W68:(lt * NH + j + 1) * W68],
                                         self.psC[2 + j][:, 0:W68]))
            em.wait("t", "sS", ns)
        self.hoist_s = em.n("sS")
        em.wait("v", "vS", em.n("vS"))
        em.fin("vS", v.tensor_copy(self.sqskst[:, 0:128], self.sq0x[:, 128 * hp:128 * (hp + 1)]))
        nv = em.fin("vS", v.tensor_copy(self.sqskst[:, 128:256], self.sk0x[:, 128 * hp:128 * (hp + 1)]))
        em.wait("v", "sS", self.hoist_s)
        for lt in range(NL):
            self.scan_step(nt_i, hp, lt, gen)
        self.emit_ytr(hp, NL - 1)

    def emit_ytr(self, hp, lt_prev):
        em, nc = self.em, self.nc
        t, s = nc.tensor, nc.scalar
        em.wait("t", "vS", self.yt_v)
        nt = em.fin("tS", t.transpose(self.psB[1][:, 128:256], self.yt[:], self.idb_sb[:]))
        em.wait("s", "tS", nt)
        ns = em.fin("sS", s.copy(self.Ytile[hp][:, 128 * lt_prev:128 * (lt_prev + 1)],
                                 self.psB[1][:, 128:256]))
        self.ytile_s = ns

    def scan_step(self, nt_i, hp, lt, gen):
        em, nc = self.em, self.nc
        v, s, t = nc.vector, nc.scalar, nc.tensor
        W = NH * E1
        V3 = lambda ap: ap.rearrange("p (h e) -> p h e", e=E1)
        if self.moe_alive:
            self.qm(gen)
        if lt > 0:
            self.emit_ytr(hp, lt - 1)
        xc = self.xcur_all[:, 128 * lt:128 * (lt + 1)]
        W2 = 2 * W
        V4 = lambda ap: ap.rearrange("p (two h e) -> p two h e", two=2, e=E1)
        xcb = xc.rearrange("p (h e) -> p h e", e=E1).unsqueeze(1).broadcast_to([128, 2, NH, E1])
        sgb = self.segred[:].rearrange("p (two h) -> p two h", two=2).unsqueeze(3).broadcast_to([128, 2, NH, E1])
        em.wait("v", "vS", em.n("vS"))
        em.fin("vS", v.tensor_tensor(V4(self.scr1[:, 0:W2]), xcb, V4(self.sqskst[:]), op=ALU.mult))
        em.wait("v", "vS", em.n("vS"))
        em.fin("vS", v.tensor_reduce(self.segred[:].rearrange("p (two h) -> p two h", two=2),
                                     V4(self.scr1[:, 0:W2]), axis=AX.X, op=ALU.max, negate=True))
        em.wait("v", "vS", em.n("vS"))
        em.fin("vS", v.tensor_tensor(V4(self.scr1[:, 0:W2]), V4(self.scr1[:, 0:W2]), sgb, op=ALU.add))
        em.wait("s", "vS", em.n("vS"))
        ns = em.fin("sS", s.activation(self.scr1[:, 0:W2], self.scr1[:, 0:W2], AF.Exp))
        em.wait("v", "sS", ns)
        em.fin("vS", v.tensor_reduce(self.segred[:].rearrange("p (two h) -> p two h", two=2),
                                     V4(self.scr1[:, 0:W2]), axis=AX.X, op=ALU.add))
        em.wait("v", "vS", em.n("vS"))
        em.fin("vS", v.reciprocal(self.segred[:], self.segred[:]))
        if self.moe_alive:
            self.qm(gen)
        em.wait("v", "vS", em.n("vS"))
        em.fin("vS", v.tensor_tensor(V4(self.qkbuf[:]), V4(self.scr1[:, 0:W2]), sgb, op=ALU.mult))
        em.wait("v", "vS", em.n("vS"))
        em.fin("vS", v.tensor_tensor(self.dcur[:], self.qkbuf[:, 0:W], self.qkbuf[:, W:W2], op=ALU.subtract))
        em.wait("v", "vS", em.n("vS"))
        ndv = em.fin("vS", v.tensor_scalar(self.dT[:], self.dcur[:], 1.0, None, ALU.mult))
        em.fin("vS", v.tensor_copy(self.Kbuf[:, W * lt:W * (lt + 1)], self.qkbuf[:, W:W2]))
        em.wait("t", "vS", ndv)
        em.wait("t", "vS", getattr(self, "dTe_s", 0))
        nt = em.fin("tS", t.transpose(self.psB[1][:, 0:128], self.dT[:], self.idb_sb[:]))
        em.wait("v", "tS", nt)
        ns = em.fin("vS", v.tensor_copy(self.dTe[:], self.psB[1][:, 0:128]))
        self.dTe_s = ns
        em.wait("t", "vS", ns)
        em.wait("t", "vS", getattr(self, "diffs_s", 0))
        for j in range(NH):
            h = NH * hp + j
            base = 64 * j
            em.fin("tS", t.matmul(self.psC[2 + j][:, 0:W196],
                                  self.dTe[base:base + 64, 0:128],
                                  self.wcat_sb[base:base + 64, W196 * h:W196 * (h + 1)],
                                  start=True, stop=True))
        ntw = em.n("tS")
        em.wait("v", "tS", ntw)
        for j in range(NH):
            nsd = em.fin("vS", v.tensor_copy(self.diffs[:, W196 * j:W196 * (j + 1)], self.psC[2 + j][:, 0:W196]))
        self.diffs_s = nsd
        if self.moe_alive:
            self.qm(gen)
        Y3 = self.y0ball[:, lt * NH * W68:(lt + 1) * NH * W68].rearrange("p (h w) -> p h w", w=W68)
        if lt > 0:
            KB = self.Kbuf[:].rearrange("p (sg h e) -> p sg h e", h=NH, e=E1)
            xb = xc.rearrange("p (h e) -> p h e", e=E1).unsqueeze(1).broadcast_to([128, lt, NH, E1])
            db = self.dcur[:].rearrange("p (h e) -> p h e", e=E1).unsqueeze(1).broadcast_to([128, lt, NH, E1])
            S1 = self.scr1[:, 0:lt * W].rearrange("p (sg h e) -> p sg h e", h=NH, e=E1)
            em.wait("v", "vS", em.n("vS"))
            em.fin("vS", v.tensor_tensor(S1, KB[:, 0:lt, :, :], xb, op=ALU.mult))
            em.wait("v", "vS", em.n("vS"))
            em.fin("vS", v.tensor_reduce(self.adots[:, 0:lt * NH].rearrange("p (sg h) -> p sg h", h=NH),
                                         S1, axis=AX.X, op=ALU.add))
            em.wait("v", "vS", em.n("vS"))
            em.fin("vS", v.tensor_tensor(S1, KB[:, 0:lt, :, :], db, op=ALU.mult))
            em.wait("v", "vS", em.n("vS"))
            em.fin("vS", v.tensor_reduce(self.bdots[:, 0:lt * NH].rearrange("p (sg h) -> p sg h", h=NH),
                                         S1, axis=AX.X, op=ALU.add))
            if self.moe_alive:
                self.qm(gen)
            bsrc = self.bdots[:, 0:lt * NH].rearrange("p (sg h) -> p sg h", h=NH)
            bsrc = bsrc.transpose([0, 2, 1]).unsqueeze(2).broadcast_to([128, NH, 3, lt])
            em.wait("v", "vS", em.n("vS"))
            em.fin("vS", v.tensor_scalar(
                self.bexp3[:, 0:NH * 3 * lt].rearrange("p (h d sg) -> p h d sg", d=3, sg=lt),
                bsrc, 1.0, None, ALU.mult))
            U4 = self.Ubuf[:].rearrange("p (hd e sg) -> p hd e sg", hd=NH * 3, e=E1)
            S2 = self.scr1[:, 0:NH * 3 * E1 * lt].rearrange("p (hd e sg) -> p hd e sg", hd=NH * 3, e=E1)
            bap = self.bexp3[:, 0:NH * 3 * lt].rearrange("p (hd sg) -> p hd sg", hd=NH * 3)
            bap = bap.unsqueeze(2).broadcast_to([128, NH * 3, E1, lt])
            em.wait("v", "vS", em.n("vS"))
            em.fin("vS", v.tensor_tensor(S2, U4[:, :, :, 0:lt], bap, op=ALU.mult))
            em.wait("v", "vS", em.n("vS"))
            em.fin("vS", v.tensor_reduce(self.corr3[:].rearrange("p (hd e) -> p hd e", e=E1),
                                         S2, axis=AX.X, op=ALU.add))
            if self.moe_alive:
                self.qm(gen)
            aap = self.adots[:, 0:lt * NH].rearrange("p (sg h) -> p sg h", h=NH)
            aap_y = aap.transpose([0, 2, 1]).unsqueeze(2).broadcast_to([128, NH, E1, lt])
            UY = self.Ubuf[:].rearrange("p (h d esg) -> p h d esg", h=NH, d=3)[:, :, 0, :]
            UY = UY.rearrange("p h (e sg) -> p h e sg", e=E1)
            S1y = self.scr1[:, 0:NH * E1 * lt].rearrange("p (h e sg) -> p h e sg", h=NH, e=E1)
            em.wait("v", "vS", em.n("vS"))
            em.fin("vS", v.tensor_tensor(S1y, UY[:, :, :, 0:lt], aap_y, op=ALU.mult))
            em.wait("v", "vS", em.n("vS"))
            em.fin("vS", v.tensor_reduce(self.ycorr[:].rearrange("p (h e) -> p h e", e=E1),
                                         S1y, axis=AX.X, op=ALU.add))
            if self.moe_alive:
                self.qm(gen)
            UB = self.Ubbuf[:].rearrange("p (h f sg) -> p h f sg", h=NH, f=4)
            aap_b = aap.transpose([0, 2, 1]).unsqueeze(2).broadcast_to([128, NH, 4, lt])
            S1b = self.scr1[:, NH * E1 * lt:NH * E1 * lt + NH * 4 * lt].rearrange(
                "p (h f sg) -> p h f sg", h=NH, f=4)
            em.wait("v", "vS", em.n("vS"))
            em.fin("vS", v.tensor_tensor(S1b, UB[:, :, :, 0:lt], aap_b, op=ALU.mult))
            em.wait("v", "vS", em.n("vS"))
            em.fin("vS", v.tensor_reduce(self.bcorr[:].rearrange("p (h f) -> p h f", f=4),
                                         S1b, axis=AX.X, op=ALU.add))
            em.wait("v", "vS", em.n("vS"))
            self.yt_v = em.fin("vS", v.tensor_tensor(self.yt[:].rearrange("p (h e) -> p h e", e=E1),
                                                     Y3[:, :, 0:E1],
                                                     self.ycorr[:].rearrange("p (h e) -> p h e", e=E1),
                                                     op=ALU.add))
            em.wait("v", "vS", em.n("vS"))
            nv = em.fin("vS", v.tensor_tensor(self.beta_t[:].rearrange("p (h f) -> p h f", f=4),
                                              Y3[:, :, E1:W68],
                                              self.bcorr[:].rearrange("p (h f) -> p h f", f=4),
                                              op=ALU.add))
        else:
            em.wait("v", "vS", em.n("vS"))
            self.yt_v = em.fin("vS", v.tensor_copy(self.yt[:].rearrange("p (h e) -> p h e", e=E1),
                                                   Y3[:, :, 0:E1]))
            nv = em.fin("vS", v.tensor_copy(self.beta_t[:].rearrange("p (h f) -> p h f", f=4),
                                            Y3[:, :, E1:W68]))
        em.wait("s", "vS", nv)
        ns = em.fin("sS", s.activation(self.beta_s[:], self.beta_t[:], AF.Tanh, scale=0.5))
        em.wait("v", "sS", ns)
        nv = em.fin("vS", v.tensor_scalar(self.beta_s[:], self.beta_s[:], 0.5, 0.5, ALU.mult, ALU.add))
        em.wait("v", "vS", self.diffs_s)
        D3 = self.diffs[:].rearrange("p (h w) -> p h w", w=W196)
        if lt > 0:
            em.wait("v", "vS", em.n("vS"))
            em.fin("vS", v.tensor_tensor(D3[:, :, 0:3 * E1].rearrange("p h (d e) -> p h d e", e=E1),
                                         D3[:, :, 0:3 * E1].rearrange("p h (d e) -> p h d e", e=E1),
                                         self.corr3[:].rearrange("p (h d e) -> p h d e", d=3, e=E1),
                                         op=ALU.add))
            em.wait("v", "vS", em.n("vS"))
            em.fin("vS", v.tensor_tensor(D3[:, :, 3 * E1:W196],
                                         D3[:, :, 3 * E1:W196],
                                         self.bcorr[:].rearrange("p (h f) -> p h f", f=4),
                                         op=ALU.add))
        U4 = self.Ubuf[:].rearrange("p (h d e sg) -> p h d e sg", h=NH, d=3, e=E1)
        bexp = self.beta_s[:].rearrange("p (h f) -> p h f", f=4)[:, :, 0:3]
        bexp = bexp.unsqueeze(3).broadcast_to([128, NH, 3, E1])
        em.wait("v", "vS", em.n("vS"))
        em.fin("vS", v.tensor_tensor(U4[:, :, :, :, lt],
                                     D3[:, :, 0:3 * E1].rearrange("p h (d e) -> p h d e", e=E1),
                                     bexp, op=ALU.mult))
        UB = self.Ubbuf[:].rearrange("p (h f sg) -> p h f sg", h=NH, f=4)
        b3 = self.beta_s[:].rearrange("p (h f) -> p h f", f=4)[:, :, 3:4].broadcast_to([128, NH, 4])
        em.wait("v", "vS", em.n("vS"))
        em.fin("vS", v.tensor_tensor(UB[:, :, :, lt], D3[:, :, 3 * E1:W196], b3, op=ALU.mult))
        if self.moe_alive:
            self.qm(gen)
        em.wait("v", "vS", em.n("vS"))
        em.fin("vS", v.tensor_reduce(self.segred[:].rearrange("p (h two) -> p h two", two=2),
                                     U4[:, :, 1:3, :, lt], axis=AX.X, op=ALU.add))
        kb2 = self.qkbuf[:, W:2 * W].rearrange("p (h e) -> p h e", e=E1).unsqueeze(1).broadcast_to([128, 2, NH, E1])
        coef = self.segred[:].rearrange("p (h two) -> p h two", two=2).transpose([0, 2, 1])
        coef = coef.unsqueeze(3).broadcast_to([128, 2, NH, E1])
        em.wait("v", "vS", em.n("vS"))
        em.fin("vS", v.tensor_tensor(self.scr1[:, 0:2 * W].rearrange("p (two h e) -> p two h e", two=2, e=E1),
                                     kb2, coef, op=ALU.mult))
        em.wait("v", "vS", em.n("vS"))
        em.fin("vS", v.tensor_tensor(self.sqskst[:], self.sqskst[:], self.scr1[:, 0:2 * W], op=ALU.add))

    def wout_spill(self, nt_i, gen):
        em, nc = self.em, self.nc
        v, s, y, t = nc.vector, nc.scalar, nc.sync, nc.tensor
        px = self.px
        NL = self.NL
        em.wait("t", px + "dwo", self.wout_ld)
        em.wait("t", "sS", self.ytile_s)
        for th in range(2 if NL == 8 else 1):
            ncols = 512 if NL == 8 else NL * 128
            cols = slice(512 * th, 512 * th + ncols)
            base_lt = th * 4 if NL == 8 else 0
            nlt = 4 if NL == 8 else NL
            for oc in range(8):
                pc = self.psC[2 + oc % 2]
                em.wait("t", "vS", getattr(self, "s1_drain_v", 0))
                for cc in range(8):
                    em.fin("tS", t.matmul(pc[:, 0:ncols],
                                          self.woutsb[:, 1024 * cc + 128 * oc:1024 * cc + 128 * (oc + 1)],
                                          self.Ytile[cc][:, cols],
                                          start=(cc == 0), stop=(cc == 7)))
                nt = em.n("tS")
                em.wait("v", "tS", nt)
                xn3 = self.xn_d[oc][:, :].rearrange("p (lt n) -> p lt n", n=SC)
                xn3 = xn3[:, base_lt:base_lt + nlt, 128 * nt_i:128 * nt_i + 128]
                nv = em.fin("vS", v.scalar_tensor_tensor(
                    self.s1pre[:, 512 * oc:512 * oc + ncols].rearrange("p (lt n) -> p lt n", n=128),
                    pc[:, 0:ncols].rearrange("p (lt n) -> p lt n", n=128), 1.0,
                    xn3, ALU.mult, ALU.add))
                self.s1_drain_v = nv
                if oc % 4 == 3 and self.moe_alive:
                    self.qm(gen)
            for li in range(nlt):
                lt = base_lt + li
                ti = 4 * lt + nt_i
                pb = self.psB[li % 2]
                em.wait("t", "vS", self.s1_drain_v)
                em.wait("t", "sS", getattr(self, "s1tile_s", 0))
                em.wait("s", px + "ds1", getattr(self, "s1_spill_d", 0))
                for oc in range(8):
                    nt = em.fin("tS", t.transpose(pb[:, 128 * (oc % 4):128 * (oc % 4) + 128],
                                                  self.s1pre[:, 512 * oc + 128 * li:512 * oc + 128 * (li + 1)],
                                                  self.idb_sb[:]))
                    if oc % 4 == 3:
                        em.wait("s", "tS", nt)
                        for o2 in range(oc - 3, oc + 1):
                            ns = em.fin("sS", s.copy(self.s1tile[:, 128 * o2:128 * (o2 + 1)],
                                                     pb[:, 128 * (o2 % 4):128 * (o2 % 4) + 128]))
                        em.wait("t", "sS", ns)
                em.wait("s", "sS", em.n("sS"))
                em.fin("sS", s.activation(self.act_scr[:, :], self.s1tile[:, :],
                                          AF.Identity, accum_out=self.statsall[:, 2 * ti:2 * ti + 1]))
                ns = em.fin("sS", s.activation(self.act_scr2[:, :], self.s1tile[:, :],
                                               AF.Square, accum_out=self.statsall[:, 2 * ti + 1:2 * ti + 2]))
                self.s1tile_s = ns
                em.wait("y", "sS", ns)
                nd = em.fin(px + "ds1", y.dma_start(self.s1p_hbm[128 * ti:128 * (ti + 1), :],
                                                    self.s1tile[:]), 16)
                self.s1_spill_d = nd
                if self.moe_alive:
                    self.qm(gen)

    def g1_final(self):
        em, nc = self.em, self.nc
        v, s, y, t = nc.vector, nc.scalar, nc.sync, nc.tensor
        px = self.px
        T = self.tiny
        em.wait("v", "sS", em.n("sS"))
        NTl = self.NTL
        sa = self.statsall[:, 0:2 * NTl].rearrange("p (t two) -> p t two", two=2)
        nv = em.fin("v", v.tensor_scalar(self.meanall[:, 0:NTl], sa[:, :, 0], 1.0 / D, None, ALU.mult))
        em.fin("v", v.tensor_scalar(self.rstdall[:, 0:NTl], sa[:, :, 1], 1.0 / D, None, ALU.mult))
        em.wait("v", "v", em.n("v"))
        em.fin("v", v.tensor_tensor(T[:, 0:NTl], self.meanall[:, 0:NTl],
                                    self.meanall[:, 0:NTl], op=ALU.mult))
        em.wait("v", "v", em.n("v"))
        nv = em.fin("v", v.tensor_tensor(self.rstdall[:, 0:NTl], self.rstdall[:, 0:NTl],
                                         T[:, 0:NTl], op=ALU.subtract))
        em.wait("s", "v", nv)
        ns = em.fin("s", s.activation(self.rstdall[:, 0:NTl], self.rstdall[:, 0:NTl], AF.Sqrt,
                                      bias=self.cst_sb[:, 0:1]))
        em.wait("v", "s", ns)
        nv = em.fin("v", v.reciprocal(self.rstdall[:, 0:NTl], self.rstdall[:, 0:NTl]))
        em.wait("y", px + "drt", self.moe_done_dma)
        for tg in range(self.NL):
            for oc in range(8):
                p = oc % 2
                em.wait("y", "s", getattr(self, f"rld_rd_{p}", 0))
                nd = em.fin(px + f"drr{p}", y.dma_start(
                    self.rld2[p][:], self.routed_hbm[128 * oc:128 * (oc + 1), SC * tg:SC * (tg + 1)]), 16)
                em.wait("s", px + f"drr{p}", nd)
                em.wait("s", "t", getattr(self, "g1tp_t", 0))
                ns = em.fin("s", s.activation(self.silu_b[:], self.rld2[p][:], AF.Silu))
                setattr(self, f"rld_rd_{p}", ns)
                em.wait("t", "s", ns)
                em.wait("t", "v", getattr(self, "g1t4_v", 0))
                pb = self.psB[oc % 2]
                for i in range(4):
                    nt = em.fin("t", t.transpose(pb[:, 128 * i:128 * (i + 1)],
                                                 self.silu_b[:, 128 * i:128 * (i + 1)],
                                                 self.idb_sb[:]))
                self.g1tp_t = em.n("t")
                em.wait("v", "t", self.g1tp_t)
                for i in range(4):
                    nv = em.fin("v", v.tensor_copy(self.g1t4[i][:, 128 * oc:128 * (oc + 1)],
                                                   pb[:, 128 * i:128 * (i + 1)]))
                self.g1t4_v = nv
            for i in range(4):
                ti = 4 * tg + i
                p = i % 2
                em.wait("y", "v", getattr(self, f"fin_v_{p}", 0))
                em.fin(px + f"dfs{p}", y.dma_start(self.s1ld[p][:], self.s1p_hbm[128 * ti:128 * (ti + 1), :]), 16)
                nd = em.fin(px + f"dfs{p}", y.dma_start(self.xld[p][:], self.x1[128 * ti:128 * (ti + 1), :]), 16)
                em.wait("v", px + f"dfs{p}", nd)
                em.wait("v", "v", em.n("v"))
                em.wait("v", px + "dout", getattr(self, "out_d", 0))
                em.fin("v", v.tensor_scalar(self.outt[:], self.s1ld[p][:],
                                            self.meanall[:, ti:ti + 1], self.rstdall[:, ti:ti + 1],
                                            ALU.subtract, ALU.mult))
                em.wait("v", "v", em.n("v"))
                em.fin("v", v.tensor_tensor(self.outt[:], self.outt[:], self.gx[:], op=ALU.mult))
                em.wait("v", "v", em.n("v"))
                em.fin("v", v.tensor_tensor(self.outt[:], self.outt[:], self.bx[:], op=ALU.add))
                em.wait("v", "v", em.n("v"))
                em.fin("v", v.tensor_tensor(self.outt[:], self.outt[:], self.g1t4[i][:], op=ALU.mult))
                em.wait("v", "v", em.n("v"))
                nv = em.fin("v", v.tensor_tensor(self.outt[:], self.outt[:], self.xld[p][:], op=ALU.add))
                setattr(self, f"fin_v_{p}", nv)
                em.wait("y", "v", nv)
                self.out_d = em.fin(px + "dout", y.dma_start(self.outo[128 * ti:128 * (ti + 1), :], self.outt[:]), 16)
            em.wait("v", "v", em.n("v"))
        em.wait("y", px + "dout", em.n(px + "dout"))

from contextlib import ExitStack


def _build_fused():
    nc = bass.Bass("TRN2", target_bir_lowering=False)
    with ExitStack() as outer:
        em = Emit(nc, outer)
        with ExitStack() as ctx0:
            l0 = L0(nc, em, ctx0, n_cores=N_CORES, NB=B, x1_internal=True)
            l0.emit()
        with ExitStack() as ctx1:
            l1 = L1(nc, em, ctx1, n_cores=N_CORES, NL=B, x1_ap=l0.x1o)
            l1.emit()
    return nc


def _l1_host_inputs(core, wy, wq, wk, wb, w_out, gamma, beta, gate_w,
                    swg, swu, swd, ewg, ewu, ewd):
    f = np.float32
    wy0b = np.zeros((128, H1 * 68), f)
    wcat = np.zeros((128, H1 * 196), f)
    wq0T = np.zeros((128, H1 * 64), f)
    wk0T = np.zeros((128, H1 * 64), f)
    for h in range(H1):
        r = (h % 2) * 64
        wy0b[r:r + 64, h * 68:h * 68 + 64] = wy[0, h]
        wy0b[r:r + 64, h * 68 + 64:h * 68 + 68] = wb[0, h]
        wcat[r:r + 64, h * 196:h * 196 + 64] = wy[0, h]
        wcat[r:r + 64, h * 196 + 64:h * 196 + 128] = wq[0, h]
        wcat[r:r + 64, h * 196 + 128:h * 196 + 192] = wk[0, h]
        wcat[r:r + 64, h * 196 + 192:h * 196 + 196] = wb[0, h]
        wq0T[r:r + 64, h * 64:(h + 1) * 64] = np.ascontiguousarray(wq[0, h].T)
        wk0T[r:r + 64, h * 64:(h + 1) * 64] = np.ascontiguousarray(wk[0, h].T)
    gw = np.ascontiguousarray(np.asarray(gate_w, f).reshape(8, 128, EXP)
                              .transpose(1, 0, 2).reshape(128, 8 * EXP))
    return {
        "l1_wy0b": wy0b, "l1_wcat": wcat, "l1_wq0T": wq0T,
        "l1_wk0T": wk0T, "l1_woutT": np.ascontiguousarray(np.asarray(w_out, f).T),
        "l1_gamma": np.ascontiguousarray(np.asarray(gamma, f).reshape(1, D)),
        "l1_beta": np.ascontiguousarray(np.asarray(beta, f).reshape(1, D)),
        "l1_gatew": gw,
        "l1_swg": np.ascontiguousarray(swg, dtype=f),
        "l1_swu": np.ascontiguousarray(swu, dtype=f),
        "l1_swd": np.ascontiguousarray(swd, dtype=f),
        "l1_ewg": np.ascontiguousarray(ewg, dtype=f),
        "l1_ewu": np.ascontiguousarray(ewu, dtype=f),
        "l1_ewd": np.ascontiguousarray(ewd, dtype=f),
        "l1_idb": np.eye(128, dtype=ml_dtypes.bfloat16),
        "l1_onesrow": np.ones((1, 128), ml_dtypes.bfloat16),
        "l1_onescol": np.ones((128, 1), ml_dtypes.bfloat16),
        "l1_cst": np.tile(np.array([[1e-5, 1e-8, 0.0, 1.0]], np.float32), (128, 1)),
    }


def kernel(L0_attn_hidden_states, L0_attn_Wq, L0_attn_Wk, L0_attn_Wv,
           L0_attn_Wg, L0_attn_Wo, L0_attn_lepe_w, L0_attn_lepe_b,
           L0_ffn_w_up, L0_ffn_w_down,
           L1_attn_weight_y, L1_attn_weight_q, L1_attn_weight_k,
           L1_attn_weight_beta, L1_attn_w_out, L1_attn_ln_gamma,
           L1_attn_ln_beta, L1_ffn_gate_w, L1_ffn_sw_gate, L1_ffn_sw_up,
           L1_ffn_sw_down, L1_ffn_ew_gate, L1_ffn_ew_up, L1_ffn_ew_down):
    x = np.asarray(L0_attn_hidden_states, np.float32)
    nc = _build_fused()
    in_maps = []
    for c in range(N_CORES):
        m = l0_host_inputs(c, x, L0_attn_Wq, L0_attn_Wk, L0_attn_Wv,
                           L0_attn_Wg, L0_attn_Wo, L0_ffn_w_up,
                           L0_ffn_w_down, L0_attn_lepe_w, L0_attn_lepe_b)
        m.update(_l1_host_inputs(c, L1_attn_weight_y, L1_attn_weight_q,
                                 L1_attn_weight_k, L1_attn_weight_beta,
                                 L1_attn_w_out, L1_attn_ln_gamma,
                                 L1_attn_ln_beta, L1_ffn_gate_w,
                                 L1_ffn_sw_gate, L1_ffn_sw_up, L1_ffn_sw_down,
                                 L1_ffn_ew_gate, L1_ffn_ew_up, L1_ffn_ew_down))
        in_maps.append(m)
    res = run_bass_kernel_spmd(nc, in_maps, core_ids=list(range(N_CORES)))
    out = np.empty((B, S, D), np.float32)
    for c in range(N_CORES):
        out[:, c * SC:(c + 1) * SC, :] = np.asarray(
            res.results[c]["l1_outo"]).reshape(B, SC, D)
    return out
